# version 12
# baseline (speedup 1.0000x reference)
import numpy as np
from contextlib import ExitStack

import concourse.bass as bass
import concourse.mybir as mybir
from concourse.bass_utils import run_bass_kernel_spmd
from concourse.aot_env import allow_ici_all_to_all

F32 = mybir.dt.float32
BF16 = mybir.dt.bfloat16
AF = mybir.ActivationFunctionType
ALU = mybir.AluOpType
AXX = mybir.AxisListType.X

NCORES = 8
T = 8192
D = 2048
NST = 32
RK = 64
DSL = D // NCORES
TSL = T // NCORES
NDG = DSL // 128
NGG = D // 128
TC = 2048
NTC = T // TC
NIT = NTC * NST
MAGIC = 12582912.0
EPS = 1e-5


def build_nc(debug=False):
    nc = bass.Bass("TRN2", target_bir_lowering=False)

    xdsT = nc.dram_tensor("xdsT", [DSL, T], F32, kind="ExternalInput").ap()
    xtsT = nc.dram_tensor("xtsT", [D, TSL], F32, kind="ExternalInput").ap()
    wbcT = nc.dram_tensor("wbcT", [DSL, 128], F32, kind="ExternalInput").ap()
    wdtT = nc.dram_tensor("wdtT", [RK, DSL], F32, kind="ExternalInput").ap()
    alog = nc.dram_tensor("alog", [DSL, NST], F32, kind="ExternalInput").ap()
    bdt = nc.dram_tensor("bdt", [128, NDG], F32, kind="ExternalInput").ap()
    dpar = nc.dram_tensor("dpar", [128, NDG], F32, kind="ExternalInput").ap()
    lnw = nc.dram_tensor("lnw", [128, NGG], F32, kind="ExternalInput").ap()
    lnb = nc.dram_tensor("lnb", [128, NGG], F32, kind="ExternalInput").ap()
    wmat = nc.dram_tensor("wmat", [D, D], F32, kind="ExternalInput").ap()
    ident16 = nc.dram_tensor("ident16", [128, 128], BF16, kind="ExternalInput").ap()
    ones_k16 = nc.dram_tensor("ones_k16", [128, 1], BF16, kind="ExternalInput").ap()
    ones_kf = nc.dram_tensor("ones_kf", [128, 1], F32, kind="ExternalInput").ap()
    ones_mf = nc.dram_tensor("ones_mf", [1, 128], F32, kind="ExternalInput").ap()
    zer16 = nc.dram_tensor("zer16", [128, 512], BF16, kind="ExternalInput").ap()
    zerf = nc.dram_tensor("zerf", [128, 1], F32, kind="ExternalInput").ap()

    yout = nc.dram_tensor("yout", [D, TSL], F32, kind="ExternalOutput").ap()

    ar_i = nc.dram_tensor("ar_i", [128, T], F32)
    ar_o = nc.dram_tensor("ar_o", [128, T], F32)
    a2a_i = nc.dram_tensor("a2a_i", [NCORES, NDG, 128, TSL], BF16)
    a2a_o = nc.dram_tensor("a2a_o", [NCORES, NDG, 128, TSL], BF16)
    wq_d = nc.dram_tensor("wq_d", [NGG, 128, D], BF16)
    scr_d = nc.dram_tensor("scr_d", [128, 1], F32)
    bar_i = nc.dram_tensor("bar_i", [1, 512], F32)
    bar_o = nc.dram_tensor("bar_o", [1, 512], F32)
    if debug:
        dbg_dbc = nc.dram_tensor("dbg_dbc", [128, T], F32, kind="ExternalOutput").ap()
        dbg_dl = nc.dram_tensor("dbg_dl", [NDG, 128, T], BF16, kind="ExternalOutput").ap()
        dbg_w16 = nc.dram_tensor("dbg_w16", [NDG, 128, T], BF16, kind="ExternalOutput").ap()
        dbg_yi = nc.dram_tensor("dbg_yi", [NCORES, NDG, 128, TSL], BF16, kind="ExternalOutput").ap()
        dbg_yo = nc.dram_tensor("dbg_yo", [NCORES, NDG, 128, TSL], BF16, kind="ExternalOutput").ap()
        dbg_wq = nc.dram_tensor("dbg_wq", [NGG, 128, D], BF16, kind="ExternalOutput").ap()
        dbg_x1 = nc.dram_tensor("dbg_x1", [NGG, 128, TSL], F32, kind="ExternalOutput").ap()
        dbg_xn = nc.dram_tensor("dbg_xn", [NGG, 128, TSL], BF16, kind="ExternalOutput").ap()
        dbg_r2 = nc.dram_tensor("dbg_r2", [NGG, 128, TSL], F32, kind="ExternalOutput").ap()
        dbg_yr = nc.dram_tensor("dbg_yr", [NGG, 128, TSL], F32, kind="ExternalOutput").ap()
        dbg_v = nc.dram_tensor("dbg_v", [1, 3 * TSL], F32, kind="ExternalOutput").ap()
        dbg_mi = nc.dram_tensor("dbg_mi", [128, 2 * TSL], F32, kind="ExternalOutput").ap()

    ex = ExitStack()
    E = ex.enter_context

    s_An = [E(nc.sbuf_tensor(f"s_An{g}", [128, NST], F32)) for g in range(NDG)]
    s_bdt = E(nc.sbuf_tensor("s_bdt", [128, NDG], F32))
    s_dpar = E(nc.sbuf_tensor("s_dpar", [128, NDG], F32))
    s_lnw = E(nc.sbuf_tensor("s_lnw", [128, NGG], F32))
    s_lnb = E(nc.sbuf_tensor("s_lnb", [128, NGG], F32))
    s_id16 = E(nc.sbuf_tensor("s_id16", [128, 128], BF16))
    s_ok16 = E(nc.sbuf_tensor("s_ok16", [128, 1], BF16))
    s_okf = E(nc.sbuf_tensor("s_okf", [128, 1], F32))
    s_omf = E(nc.sbuf_tensor("s_omf", [1, 128], F32))
    s_sinv = E(nc.sbuf_tensor("s_sinv", [128, 1], F32))
    s_z16 = E(nc.sbuf_tensor("s_z16", [128, 512], BF16))
    s_zf = E(nc.sbuf_tensor("s_zf", [128, 1], F32))

    sems = {}
    for name in ["ld", "xc", "wld", "qn1", "qn2", "wqs", "dbc", "ste", "sse",
                 "sve", "svf", "ars", "ccs", "repB", "repC", "dAs", "vsc",
                 "tmm", "ysv", "ya", "yld", "wql", "fca", "fsq", "fte", "fvv",
                 "fmm", "ost", "dsm", "dbgs", "wldo", "lnr",
                 "rB0", "rB1", "rB2", "rB3", "rC0", "rC1", "rC2", "rC3"]:
        sems[name] = E(nc.semaphore(name))
    ld, xc, wld = sems["ld"], sems["xc"], sems["wld"]
    qn1, qn2, wqs = sems["qn1"], sems["qn2"], sems["wqs"]
    dbc, ste, sse = sems["dbc"], sems["ste"], sems["sse"]
    sve, svf, ars = sems["sve"], sems["svf"], sems["ars"]
    ccs, repB, repC = sems["ccs"], sems["repB"], sems["repC"]
    dAs, vsc, tmm = sems["dAs"], sems["vsc"], sems["tmm"]
    ysv, ya, yld = sems["ysv"], sems["ya"], sems["yld"]
    wql, fca, fsq = sems["wql"], sems["fca"], sems["fsq"]
    fte, fvv, fmm = sems["fte"], sems["fvv"], sems["fmm"]
    ost = sems["ost"]
    dsm = sems["dsm"]
    dbgs = sems["dbgs"]
    wldo = sems["wldo"]
    lnr = sems["lnr"]
    repBs = [sems[f"rB{i}"] for i in range(4)]
    repCs = [sems[f"rC{i}"] for i in range(4)]

    ps = ExitStack()
    PS = ps.enter_context
    s_u16 = [PS(nc.sbuf_tensor(f"s_u16{g}", [128, T], BF16)) for g in range(NDG)]
    s_w16 = [PS(nc.sbuf_tensor(f"s_w16{g}", [128, T], BF16)) for g in range(NDG)]
    s_dl16 = [PS(nc.sbuf_tensor(f"s_dl16{g}", [128, T], BF16)) for g in range(NDG)]
    s_B16 = PS(nc.sbuf_tensor("s_B16", [128, TC], BF16))
    s_C16 = PS(nc.sbuf_tensor("s_C16", [128, TC], BF16))
    s_wbcT = [PS(nc.sbuf_tensor(f"s_wbcT{g}", [128, 128], BF16)) for g in range(NDG)]
    s_wdtT = PS(nc.sbuf_tensor("s_wdtT", [RK, DSL], F32))
    s_st = PS(nc.sbuf_tensor("s_st", [128, NDG * NST], F32))

    pre = ExitStack()
    PR = pre.enter_context
    s_dbcF = PR(nc.sbuf_tensor("s_dbcF", [128, T], F32))
    s_sp1 = PR(nc.sbuf_tensor("s_sp1", [128, TC], F32))
    s_sp2 = PR(nc.sbuf_tensor("s_sp2", [128, TC], F32))
    s_xld = s_sp1
    s_wt = [PR(nc.sbuf_tensor(f"s_wt{i}", [128, D], F32)) for i in range(2)]
    s_wt2 = [PR(nc.sbuf_tensor(f"s_wt2{i}", [128, D], F32)) for i in range(2)]
    s_wq = [PR(nc.sbuf_tensor(f"s_wq{i}", [128, D], BF16)) for i in range(2)]
    s_abs = PR(nc.sbuf_tensor("s_abs", [128, NGG + 3], F32))
    s_sc1 = PR(nc.sbuf_tensor("s_sc1", [1, 4], F32))
    s_row = PR(nc.sbuf_tensor("s_row", [1, 128], F32))
    p_a = PR(nc.psum_tensor("p_a", [128, 512], F32))
    p_b = PR(nc.psum_tensor("p_b", [128, 512], F32))
    p_s = PR(nc.psum_tensor("p_s", [128, 2], F32))

    N_CONST_DMA = (11 + NDG + 1) * 16
    STE0 = NGG + 4
    NCH = NDG * NTC

    _svf_n = [0]
    _svf_xb = {}
    _svf_abs = {}

    def _svf_next(kind, key):
        _svf_n[0] += 1
        if kind == "xb":
            _svf_xb[key] = _svf_n[0]
        elif kind == "abs":
            _svf_abs[key] = _svf_n[0]
        return _svf_n[0]

    _sp_order = []
    _sp_order += [("xb", (0, 0)), ("xb", (0, 1)), ("abs", (0, 0)),
                  ("xb", (0, 2)), ("xb", (0, 3)), ("abs", (0, 1))]
    for c in range(NCH - 1):
        _sp_order += [("d16", (c, 0)),
                      ("xb", (c + 1, 0)), ("xb", (c + 1, 1)), ("abs", (c + 1, 0)),
                      ("d16", (c, 1)),
                      ("xb", (c + 1, 2)), ("xb", (c + 1, 3)), ("abs", (c + 1, 1))]
    _sp_order += [("d16", (NCH - 1, 0)), ("d16", (NCH - 1, 1))]
    for kind, key in _sp_order:
        if kind != "d16":
            _svf_next(kind, key)
    _svf_n[0] = 0

    with nc.Block() as blk:
        @blk.sync
        def _(sync):
            loads = [
                (s_wdtT[:], wdtT),
                (s_bdt[:], bdt), (s_dpar[:], dpar),
                (s_lnw[:], lnw), (s_lnb[:], lnb),
                (s_id16[:], ident16), (s_ok16[:], ones_k16),
                (s_okf[:], ones_kf), (s_omf[:], ones_mf),
                (s_z16[:], zer16), (s_zf[:], zerf),
            ]
            for g in range(NDG):
                loads.append((s_An[g][:], alog[g * 128:(g + 1) * 128, :]))
            loads.append((
                s_sp2[:, 0:NDG * 128].rearrange("p (g q) -> p g q", q=128),
                wbcT.rearrange("(g p) q -> p g q", p=128)))
            for o, i in loads:
                sync.dma_start(out=o, in_=i).then_inc(ld, 16)
            for g in range(NDG):
                for q in range(NTC):
                    k = g * NTC + q
                    if k > 0:
                        sync.wait_ge(xc, k)
                    sync.dma_start(
                        out=s_xld[:],
                        in_=xdsT[g * 128:(g + 1) * 128, q * TC:(q + 1) * TC],
                    ).then_inc(ld, 16)
            for k in range(NGG):
                if k >= 2:
                    sync.wait_ge(qn1, k - 1)
                sync.dma_start(out=s_wt[k % 2][:],
                               in_=wmat[k * 128:(k + 1) * 128, :]).then_inc(wld, 16)
            for k in range(0, NGG, 2):
                j = k // 2
                if k == 0:
                    sync.wait_ge(qn1, NGG - 1)
                elif k >= 4:
                    sync.wait_ge(qn2, k - 3)
                buf = s_wt[0] if j % 2 == 0 else s_wt2[0]
                sync.dma_start(out=buf[:],
                               in_=wmat[k * 128:(k + 1) * 128, :]).then_inc(wld, 16)
            sync.wait_ge(qn2, NGG)
            for k in range(2):
                sync.dma_start(out=s_wt[k][:],
                               in_=wmat[k * 128:(k + 1) * 128, :]).then_inc(wld, 16)

        @blk.gpsimd
        def _(g):
            g.wait_ge(dbc, NGG)
            g.dma_start(out=ar_i.ap(), in_=s_dbcF[:]).then_inc(ars, 16)
            g.wait_ge(ars, 16)
            g.collective_compute(
                "AllReduce", ALU.add,
                replica_groups=[list(range(NCORES))],
                ins=[ar_i.ap().opt()], outs=[ar_o.ap().opt()],
            ).then_inc(ccs, 1)
            g.wait_ge(ccs, 1)
            g.dma_start(out=s_dbcF[:], in_=ar_o.ap()).then_inc(ars, 16)
            for k in range(NGG):
                g.wait_ge(qn2, k + 1)
                g.dma_start(out=wq_d.ap()[k], in_=s_wq[k % 2][:]).then_inc(wqs, 16)
            for k in range(2):
                g.wait_ge(qn2, NGG + k + 1)
                g.dma_start(out=wq_d.ap()[k], in_=s_wq[k][:]).then_inc(wqs, 16)

        @blk.vector
        def _(vector):
            vector.wait_ge(ld, N_CONST_DMA)
            for g in range(NDG):
                for q in range(NTC):
                    k = g * NTC + q
                    vector.wait_ge(ld, N_CONST_DMA + (k + 1) * 16)
                    vector.tensor_copy(
                        out=s_u16[g][:, q * TC:(q + 1) * TC], in_=s_xld[:]
                    ).then_inc(xc, 1)
            for g in range(NDG):
                vector.tensor_copy(out=s_wbcT[g][:],
                                   in_=s_sp2[:, g * 128:(g + 1) * 128])
            vector.wait_ge(sse, 1)
            for g in range(NDG):
                ins = vector.tensor_scalar_mul(out=s_An[g][:], in0=s_An[g][:],
                                               scalar1=-1.0)
            ins.then_inc(sve, 1)

            for tt in range(NGG):
                vector.wait_ge(ste, tt + 1)
                pt = p_a if tt % 2 == 0 else p_b
                vector.tensor_copy(out=s_dbcF[:, tt * 512:(tt + 1) * 512],
                                   in_=pt[:]).then_inc(dbc, 1)

            for k in range(NGG):
                vector.wait_ge(wld, (k + 1) * 16)
                vector.tensor_reduce(
                    out=s_abs[:, k:k + 1], in_=s_wt[k % 2][:], axis=AXX,
                    op=ALU.add, apply_absolute_value=True,
                ).then_inc(qn1, 1)
            vector.tensor_reduce(
                out=s_abs[:, NGG + 1:NGG + 2], in_=s_wt[(NGG - 1) % 2][:],
                axis=AXX, op=ALU.add, apply_absolute_value=True)
            vector.reduce_sum(out=s_abs[:, NGG:NGG + 1], in_=s_abs[:, 0:NGG],
                              axis=AXX)
            vector.reduce_sum(out=s_abs[:, NGG:NGG + 1], in_=s_abs[:, 0:NGG],
                              axis=AXX).then_inc(sve, 1)
            vector.wait_ge(ste, NGG + 2)
            vector.tensor_scalar_mul(out=s_sc1[:, 0:1], in0=p_s[0:1, 0:1],
                                     scalar1=1.0 / (D * D))
            vector.tensor_scalar_mul(out=s_sc1[:, 0:1], in0=p_s[0:1, 0:1],
                                     scalar1=1.0 / (D * D))
            vector.tensor_scalar_mul(out=s_sc1[:, 0:1], in0=p_s[0:1, 0:1],
                                     scalar1=1.0 / (D * D)).then_inc(sve, 1)
            vector.wait_ge(ste, NGG + 4)
            vector.tensor_copy(out=s_sinv[:], in_=p_s[:, 0:1])
            vector.tensor_copy(out=s_sinv[:], in_=p_s[:, 0:1])
            vector.tensor_copy(out=s_sinv[:], in_=p_s[:, 0:1])
            vector.wait_ge(ars, 32)
            for tcq in range(NTC):
                vector.tensor_copy(out=s_B16[tcq * NST:(tcq + 1) * NST, :],
                                   in_=s_dbcF[64:96, tcq * TC:(tcq + 1) * TC])
                vector.tensor_copy(out=s_C16[tcq * NST:(tcq + 1) * NST, :],
                                   in_=s_dbcF[96:128, tcq * TC:(tcq + 1) * TC])
            for kind, key in _sp_order:
                if kind == "xb":
                    c, j = key
                    g = c // NTC
                    m = 4 * c + j
                    vector.wait_ge(ste, STE0 + m + 1)
                    pt = p_a if m % 2 == 0 else p_b
                    vector.tensor_scalar(
                        out=s_sp1[:, j * 512:(j + 1) * 512], in0=pt[:],
                        scalar1=s_bdt[:, g:g + 1], scalar2=None,
                        op0=ALU.add).then_inc(svf, 1)
                elif kind == "abs":
                    c, h = key
                    hs = slice(h * 1024, (h + 1) * 1024)
                    vector.scalar_tensor_tensor(
                        out=s_sp2[:, hs], in0=s_sp1[:, hs], scalar=-1.0,
                        in1=s_sp1[:, hs],
                        op0=ALU.mult, op1=ALU.min).then_inc(svf, 1)
                else:
                    c, h = key
                    g, q = divmod(c, NTC)
                    hs = slice(h * 1024, (h + 1) * 1024)
                    vector.wait_ge(sse, 2 + 2 * c + h + 1)
                    vector.scalar_tensor_tensor(
                        out=s_dl16[g][:, q * TC + h * 1024:q * TC + (h + 1) * 1024],
                        in0=s_sp1[:, hs], scalar=0.0, in1=s_sp2[:, hs],
                        op0=ALU.max, op1=ALU.add)
            for g in range(NDG):
                vector.tensor_mul(out=s_w16[g][:], in0=s_dl16[g][:],
                                  in1=s_u16[g][:])
            for k in range(NGG):
                j = k // 2
                if k % 2 == 0:
                    vector.wait_ge(wld, (NGG + j + 1) * 16)
                    wbuf = s_wt[0] if j % 2 == 0 else s_wt2[0]
                else:
                    vector.wait_ge(wldo, (j + 1) * 16)
                    wbuf = s_wt[1] if j % 2 == 0 else s_wt2[1]
                if k >= 2:
                    vector.wait_ge(wqs, (k - 1) * 16)
                vector.tensor_scalar(out=s_sp1[:, 0:D], in0=wbuf[:],
                                     scalar1=s_sinv[:], scalar2=MAGIC,
                                     op0=ALU.mult, op1=ALU.add)
                vector.tensor_scalar(out=s_sp1[:, 0:D], in0=s_sp1[:, 0:D],
                                     scalar1=MAGIC, scalar2=1.0,
                                     op0=ALU.subtract, op1=ALU.min)
                vector.tensor_scalar_max(out=s_wq[k % 2][:], in0=s_sp1[:, 0:D],
                                         scalar1=-1.0).then_inc(qn2, 1)
            for k in range(2):
                vector.wait_ge(wld, (NGG + NGG // 2 + k + 1) * 16)
                vector.wait_ge(wqs, NGG * 16)
                vector.tensor_scalar(out=s_sp1[:, 0:D], in0=s_wt[k][:],
                                     scalar1=s_sinv[:], scalar2=MAGIC,
                                     op0=ALU.mult, op1=ALU.add)
                vector.tensor_scalar(out=s_sp1[:, 0:D], in0=s_sp1[:, 0:D],
                                     scalar1=MAGIC, scalar2=1.0,
                                     op0=ALU.subtract, op1=ALU.min)
                vector.tensor_scalar_max(out=s_wq[k][:], in0=s_sp1[:, 0:D],
                                         scalar1=-1.0).then_inc(qn2, 1)

        @blk.scalar
        def _(scalar):
            scalar.wait_ge(ld, N_CONST_DMA)
            for g in range(NDG):
                ins = scalar.activation(out=s_An[g][:], in_=s_An[g][:],
                                        func=AF.Exp)
            ins.then_inc(sse, 1)
            scalar.wait_ge(sve, 3)
            scalar.activation(out=s_sc1[:, 1:2], in_=s_sc1[:, 0:1], func=AF.Ln)
            scalar.activation(out=s_sc1[:, 1:2], in_=s_sc1[:, 0:1], func=AF.Ln)
            scalar.activation(out=s_sc1[:, 2:3], in_=s_sc1[:, 1:2],
                              func=AF.Exp, scale=-1.0)
            scalar.activation(out=s_sc1[:, 2:3], in_=s_sc1[:, 1:2],
                              func=AF.Exp, scale=-1.0).then_inc(sse, 1)
            for c in range(NCH):
                for h in range(2):
                    hs = slice(h * 1024, (h + 1) * 1024)
                    scalar.wait_ge(svf, _svf_abs[(c, h)])
                    scalar.activation(out=s_sp2[:, hs], in_=s_sp2[:, hs],
                                      func=AF.Exp)
                    scalar.activation(out=s_sp2[:, hs], in_=s_sp2[:, hs],
                                      func=AF.Ln, bias=1.0).then_inc(sse, 1)
            for k in range(1, NGG, 2):
                j = k // 2
                if k == 1:
                    scalar.wait_ge(qn1, NGG)
                elif k >= 5:
                    scalar.wait_ge(qn2, k - 3)
                buf = s_wt[1] if j % 2 == 0 else s_wt2[1]
                scalar.dma_start(out=buf[:],
                                 in_=wmat[k * 128:(k + 1) * 128, :]
                                 ).then_inc(wldo, 16)

        @blk.tensor
        def _(tensor):
            tensor.wait_ge(sve, 1)
            for tt in range(NGG):
                pt = p_a if tt % 2 == 0 else p_b
                if tt >= 2:
                    tensor.wait_ge(dbc, tt - 1)
                for g in range(NDG):
                    tensor.matmul(pt[:], s_wbcT[g][:],
                                  s_u16[g][:, tt * 512:(tt + 1) * 512],
                                  start=(g == 0), stop=False)
                tensor.matmul(pt[:], s_wbcT[0][:], s_z16[:],
                              start=False, stop=True).then_inc(ste, 1)
            tensor.wait_ge(sve, 2)
            tensor.matmul(p_s[0:1, 0:1], s_okf[:], s_abs[:, NGG:NGG + 1],
                          start=True, stop=False)
            tensor.matmul(p_s[0:1, 0:1], s_okf[:], s_zf[:, 0:1],
                          start=False, stop=False)
            tensor.matmul(p_s[0:1, 0:1], s_okf[:], s_zf[:, 0:1],
                          start=False, stop=True).then_inc(ste, 2)
            tensor.wait_ge(sse, 2)
            tensor.matmul(p_s[:, 0:1], s_omf[:], s_sc1[0:1, 2:3],
                          start=True, stop=False)
            tensor.matmul(p_s[:, 0:1], s_omf[:], s_zf[0:1, 0:1],
                          start=False, stop=False)
            tensor.matmul(p_s[:, 0:1], s_omf[:], s_zf[0:1, 0:1],
                          start=False, stop=True).then_inc(ste, 2)
            tensor.wait_ge(ars, 32)
            for m in range(NDG * NGG):
                g, tt4 = divmod(m, NGG)
                pt = p_a if m % 2 == 0 else p_b
                if m >= 2:
                    mm = m - 2
                    tensor.wait_ge(svf, _svf_xb[(mm // 4, mm % 4)])
                tensor.matmul(pt[:], s_wdtT[:, g * 128:(g + 1) * 128],
                              s_dbcF[0:RK, tt4 * 512:(tt4 + 1) * 512],
                              start=True, stop=True).then_inc(ste, 1)


    if debug:
        with nc.Block() as blk:
            @blk.gpsimd
            def _(g):
                dsem = nc.semaphore("dsem1").__enter__()
                g.dma_start(out=dbg_dbc, in_=s_dbcF[:]).then_inc(dsem, 16)
                for gg in range(NDG):
                    g.dma_start(out=dbg_dl[gg], in_=s_dl16[gg][:]).then_inc(dsem, 16)
                    g.dma_start(out=dbg_w16[gg], in_=s_w16[gg][:]).then_inc(dsem, 16)
                g.wait_ge(dsem, (1 + 2 * NDG) * 16)

    pre.close()

    sc = ExitStack()
    SC = sc.enter_context
    s_Brep = [SC(nc.sbuf_tensor(f"s_Brep{i}", [128, TC], BF16)) for i in range(4)]
    s_Crep = [SC(nc.sbuf_tensor(f"s_Crep{i}", [128, TC], BF16)) for i in range(4)]
    s_dA = [SC(nc.sbuf_tensor(f"s_dA{g}", [128, TC], F32)) for g in range(NDG)]
    s_wb = SC(nc.sbuf_tensor("s_wb", [128, TC], BF16))
    s_h = SC(nc.sbuf_tensor("s_h", [128, TC], BF16))
    s_hc = [SC(nc.sbuf_tensor(f"s_hc{i}", [128, TC], BF16)) for i in range(2)]
    s_ys = [SC(nc.sbuf_tensor(f"s_ys{g}", [128, TC], BF16)) for g in range(NDG)]
    p_y = [SC(nc.psum_tensor(f"p_y{g}", [128, TC], F32)) for g in range(NDG)]

    def chains_interleaved(eng, bufs, rows, sem, gate, it_range=None, state=None,
                           group=2):
        depth = len(bufs)
        steps = [(0, 1)]
        k = 1
        while k < 128:
            c = min(k, 128 - k)
            steps.append((k, c))
            k += c
        nst = len(steps)
        if state is None:
            state = [0]
        if it_range is None:
            it_range = range(0, NIT)
        lo, hi = it_range.start, it_range.stop
        for itg in range(lo, hi, group):
            grp = [itg + j for j in range(group) if itg + j < hi]
            for s in range(nst):
                for it in grp:
                    slot = it % depth
                    base = (it // depth) * nst * 16
                    if s == 0:
                        gate(eng, it)
                        eng.dma_start(out=bufs[slot][0:1, :],
                                      in_=rows[it]).then_inc(sem[slot], 16)
                    else:
                        k0, c = steps[s]
                        eng.wait_ge(sem[slot], base + s * 16)
                        eng.dma_start(out=bufs[slot][k0:k0 + c, :],
                                      in_=bufs[slot][0:c, :]
                                      ).then_inc(sem[slot], 16)

    with nc.Block() as blk:
        @blk.sync
        def _(sync):
            rowsB = []
            for it in range(NIT):
                tc, n = divmod(it, NST)
                rowsB.append(s_B16[tc * NST + n:tc * NST + n + 1, :])

            def gateB(eng, it):
                if it >= 4:
                    eng.wait_ge(vsc, (it - 3) * NDG)
            chains_interleaved(sync, s_Brep, rowsB, repBs, gateB)

        @blk.gpsimd
        def _(g):
            rowsC = []
            for it in range(NIT):
                tc, n = divmod(it, NST)
                rowsC.append(s_C16[tc * NST + n:tc * NST + n + 1, :])

            def gateC(eng, it):
                if it >= 4:
                    eng.wait_ge(vsc, (it - 3) * NDG)
            for tc in range(NTC):
                chains_interleaved(g, s_Crep, rowsC, repCs, gateC,
                                   it_range=range(tc * NST, (tc + 1) * NST))
                for dg in range(NDG):
                    g.wait_ge(ysv, tc * NDG + dg + 1)
                    for h in range(TC // TSL):
                        dest = tc * (TC // TSL) + h
                        g.dma_start(
                            out=a2a_i.ap()[dest, dg],
                            in_=s_ys[dg][:, h * TSL:(h + 1) * TSL],
                        ).then_inc(ya, 16)

        @blk.scalar
        def _(scalar):
            for it in range(NIT):
                tc, n = divmod(it, NST)
                for g in range(NDG):
                    idx = it * NDG + g
                    if idx >= NDG:
                        scalar.wait_ge(vsc, idx - NDG + 1)
                    scalar.activation(
                        out=s_dA[g][:], in_=s_dl16[g][:, tc * TC:(tc + 1) * TC],
                        func=AF.Exp, scale=s_An[g][:, n:n + 1],
                    ).then_inc(dAs, 1)

        @blk.vector
        def _(vector):
            for tc in range(NTC):
                for n in range(NST):
                    it = tc * NST + n
                    vector.wait_ge(repBs[it % 4], 128 * (it // 4 + 1))
                    vector.wait_ge(repCs[it % 4], 128 * (it // 4 + 1))
                    for g in range(NDG):
                        idx = it * NDG + g
                        col = g * NST + n
                        vector.wait_ge(dAs, idx + 1)
                        vector.tensor_mul(out=s_wb[:],
                                          in0=s_w16[g][:, tc * TC:(tc + 1) * TC],
                                          in1=s_Brep[it % 4][:])
                        init = 0.0 if tc == 0 else s_st[:, col:col + 1]
                        vector.tensor_tensor_scan(
                            out=s_h[:], data0=s_dA[g][:], data1=s_wb[:],
                            initial=init, op0=ALU.mult, op1=ALU.add)
                        if idx >= 2:
                            vector.wait_ge(tmm, idx - 1)
                        vector.tensor_mul(out=s_hc[idx % 2][:], in0=s_h[:],
                                          in1=s_Crep[it % 4][:]).then_inc(vsc, 1)
                        vector.tensor_copy(out=s_st[:, col:col + 1],
                                           in_=s_h[:, TC - 1:TC])
                for g in range(NDG):
                    vector.wait_ge(tmm, (tc * NST + NST - 1) * NDG + g + 1)
                    if tc >= 1:
                        vector.wait_ge(ya, (tc - 1) * NDG * 32 + (g + 1) * 32)
                    vector.scalar_tensor_tensor(
                        out=s_ys[g][:], in0=s_u16[g][:, tc * TC:(tc + 1) * TC],
                        scalar=s_dpar[:, g:g + 1], in1=p_y[g][:],
                        op0=ALU.mult, op1=ALU.add).then_inc(ysv, 1)

        @blk.tensor
        def _(tensor):
            for it in range(NIT):
                tc, n = divmod(it, NST)
                for g in range(NDG):
                    idx = it * NDG + g
                    tensor.wait_ge(vsc, idx + 1)
                    if n == 0 and tc >= 1:
                        tensor.wait_ge(ysv, (tc - 1) * NDG + g + 1)
                    for j in range(TC // 512):
                        tensor.matmul(
                            p_y[g][:, j * 512:(j + 1) * 512], s_id16[:],
                            s_hc[idx % 2][:, j * 512:(j + 1) * 512],
                            start=(n == 0), stop=False)
                    if n == NST - 1:
                        for j in range(TC // 512):
                            ins = tensor.matmul(
                                p_y[g][:, j * 512:(j + 1) * 512], s_id16[:],
                                s_z16[:], start=False, stop=True)
                        ins.then_inc(tmm, 1)
                    else:
                        ins = tensor.matmul(
                            p_y[g][:, 0:512], s_id16[:], s_z16[:],
                            start=False, stop=False)
                        ins.then_inc(tmm, 1)

    if debug:
        with nc.Block() as blk:
            @blk.gpsimd
            def _(g):
                dsem = nc.semaphore("dsem2").__enter__()
                g.dma_start(out=dbg_yi, in_=a2a_i.ap()).then_inc(dsem, 16)
                g.dma_start(out=dbg_wq, in_=wq_d.ap()).then_inc(dsem, 16)
                g.wait_ge(dsem, 32)

    sc.close()
    ps.close()

    ff = ExitStack()
    FF = ff.enter_context
    s_x1 = [FF(nc.sbuf_tensor(f"s_x1{g}", [128, TSL], F32)) for g in range(NGG)]
    s_Wq = [FF(nc.sbuf_tensor(f"s_Wq{g}", [128, D], BF16)) for g in range(NGG)]
    s_xn = [FF(nc.sbuf_tensor(f"s_xn{g}", [128, TSL], BF16)) for g in range(NGG)]
    s_yr = [FF(nc.sbuf_tensor(f"s_yr{i}", [128, TSL], BF16)) for i in range(2)]
    s_r16 = [FF(nc.sbuf_tensor(f"s_r16{i}", [128, TSL], BF16)) for i in range(2)]
    s_q16 = [FF(nc.sbuf_tensor(f"s_q16{i}", [128, TSL], BF16)) for i in range(2)]
    s_q16d = FF(nc.sbuf_tensor("s_q16d", [128, 512], BF16))
    s_mi = FF(nc.sbuf_tensor("s_mi", [128, 2 * TSL], F32))
    s_vec = FF(nc.sbuf_tensor("s_vec", [1, 3 * TSL], F32))
    s_ost = [FF(nc.sbuf_tensor(f"s_ost{i}", [128, TSL], F32)) for i in range(2)]

    LD0 = N_CONST_DMA + NDG * NTC * 16

    es1 = ExitStack()
    p_s1_1x = es1.enter_context(nc.psum_tensor("p_s1_1x", [1, TSL], F32))
    p_s2_1x = es1.enter_context(nc.psum_tensor("p_s2_1x", [1, TSL], F32))
    es0 = ExitStack()
    p_s1_0 = es0.enter_context(nc.psum_tensor("p_s1_0", [1, TSL], F32))
    p_s2_0 = es0.enter_context(nc.psum_tensor("p_s2_0", [1, TSL], F32))
    with nc.Block() as blk:
        @blk.gpsimd
        def _(g):
            g.wait_ge(ya, NTC * NDG * 2 * 16)
            with allow_ici_all_to_all():
                g.collective_compute(
                    "AllToAll", ALU.bypass,
                    replica_groups=[list(range(NCORES))],
                    ins=[a2a_i.ap().opt()], outs=[a2a_o.ap().opt()],
                ).then_inc(ccs, 1)
            g.wait_ge(ccs, 2)
            g.collective_compute(
                "AllReduce", ALU.add,
                replica_groups=[list(range(NCORES))],
                ins=[bar_i.ap().opt()],
                outs=[bar_o.ap().opt()],
            ).then_inc(ccs, 1)
            g.wait_ge(ccs, 3)
            for gg in range(NGG):
                src, dg = divmod(gg, NDG)
                if gg >= 2:
                    g.wait_ge(fvv, gg - 1)
                g.dma_start(out=s_yr[gg % 2][:],
                            in_=a2a_o.ap()[src, dg]).then_inc(yld, 16)
            g.dma_start(out=s_ost[0][:], in_=a2a_o.ap()[0, 0]).then_inc(yld, 16)

        @blk.sync
        def _(sync):
            for gg in range(NGG):
                sync.dma_start(out=s_x1[gg][:],
                               in_=xtsT[gg * 128:(gg + 1) * 128, :]).then_inc(ld, 16)
            sync.dma_start(out=s_ost[1][:],
                           in_=xtsT[0:128, :]).then_inc(ld, 16)
            sync.wait_ge(wqs, (NGG + 2) * 16)
            sync.wait_ge(fvv, NGG)
            for gg in range(NGG):
                sync.dma_start(out=s_Wq[gg][:], in_=wq_d.ap()[gg]).then_inc(wql, 16)

        @blk.vector
        def _(vector):
            for gg in range(NGG):
                vector.wait_ge(ld, LD0 + min(gg + 2, NGG + 1) * 16)
                vector.wait_ge(yld, min(gg + 2, NGG + 1) * 16)
                vector.tensor_add(out=s_x1[gg][:], in0=s_x1[gg][:],
                                  in1=s_yr[gg % 2][:]).then_inc(fvv, 1)
                if gg >= 2:
                    vector.wait_ge(fte, gg - 1)
                vector.tensor_copy(out=s_r16[gg % 2][:],
                                   in_=s_x1[gg][:]).then_inc(fca, 1)
            vector.tensor_copy(out=s_q16d[:],
                               in_=s_x1[0][:, 0:512]).then_inc(fca, 1)

        @blk.scalar
        def _(scalar):
            for gg in range(NGG):
                scalar.wait_ge(fvv, gg + 1)
                if gg >= 2:
                    scalar.wait_ge(fte, gg - 1)
                scalar.activation(out=s_q16[gg % 2][:], in_=s_x1[gg][:],
                                  func=AF.Square).then_inc(fsq, 1)
            scalar.activation(out=s_q16d[:], in_=s_x1[0][:, 0:512],
                              func=AF.Square).then_inc(fsq, 1)

        @blk.tensor
        def _(tensor):
            for gg in range(NGG):
                tensor.wait_ge(fsq, min(gg + 2, NGG + 1))
                tensor.wait_ge(fca, min(gg + 2, NGG + 1))
                for h in range(TSL // 512):
                    sl = slice(h * 512, (h + 1) * 512)
                    tensor.matmul(p_s1_0[0:1, sl], s_ok16[:],
                                  s_r16[gg % 2][:, sl],
                                  start=(gg == 0), stop=False)
                    ins = tensor.matmul(p_s2_0[0:1, sl], s_ok16[:],
                                        s_q16[gg % 2][:, sl],
                                        start=(gg == 0), stop=False)
                    if h == TSL // 512 - 1:
                        ins.then_inc(fte, 1)
            for rnd in range(2):
                for h2 in range(TSL // 512):
                    sl2 = slice(h2 * 512, (h2 + 1) * 512)
                    tensor.matmul(p_s1_0[0:1, sl2], s_ok16[:], s_z16[:],
                                  start=False, stop=(rnd == 1))
                    tensor.matmul(p_s2_0[0:1, sl2], s_ok16[:], s_z16[:],
                                  start=False, stop=(rnd == 1))
        if debug:
            @blk.gpsimd
            def _(g):
                g.wait_ge(fvv, NGG)
                for gg in range(NGG):
                    g.dma_start(out=dbg_yr[gg], in_=s_x1[gg][:]).then_inc(yld, 16)
    if debug:
        with nc.Block() as blk:
            @blk.gpsimd
            def _(g):
                dsem = nc.semaphore("dsem3").__enter__()
                g.dma_start(out=dbg_yo, in_=a2a_o.ap()).then_inc(dsem, 16)
                g.wait_ge(dsem, 16)

    def emit_ln(idx, mode, ext_stats=None, fuse_next=None):
        off = idx * NGG
        off17 = idx * (NGG + 1)
        offn = (idx + 1) * NGG
        off17n = (idx + 1) * (NGG + 1)
        with ExitStack() as _les:
            if ext_stats is not None:
                p_s1, p_s2, _sst = ext_stats
            else:
                _sst = ExitStack()
                p_s1 = _sst.enter_context(
                    nc.psum_tensor(f"p_s1_{idx}", [1, TSL], F32))
                p_s2 = _sst.enter_context(
                    nc.psum_tensor(f"p_s2_{idx}", [1, TSL], F32))
            if ext_stats is None:
              with nc.Block() as blk:
                @blk.scalar
                def _(scalar):
                    for gg in range(NGG):
                        if gg >= 2:
                            scalar.wait_ge(fte, off + gg - 1)
                        scalar.activation(out=s_q16[gg % 2][:], in_=s_x1[gg][:],
                                          func=AF.Square).then_inc(fsq, 1)
                    scalar.activation(out=s_q16d[:], in_=s_x1[0][:, 0:512],
                                      func=AF.Square).then_inc(fsq, 1)

                @blk.vector
                def _(vector):
                    for gg in range(NGG):
                        if gg >= 2:
                            vector.wait_ge(fte, off + gg - 1)
                        vector.tensor_copy(out=s_r16[gg % 2][:],
                                           in_=s_x1[gg][:]).then_inc(fca, 1)
                    vector.tensor_copy(out=s_q16d[:],
                                       in_=s_x1[0][:, 0:512]).then_inc(fca, 1)

                @blk.tensor
                def _(tensor):
                    for gg in range(NGG):
                        tensor.wait_ge(fsq, off17 + min(gg + 2, NGG + 1))
                        tensor.wait_ge(fca, off17 + min(gg + 2, NGG + 1))
                        for h in range(TSL // 512):
                            sl = slice(h * 512, (h + 1) * 512)
                            tensor.matmul(p_s1[0:1, sl], s_ok16[:],
                                          s_r16[gg % 2][:, sl],
                                          start=(gg == 0), stop=False)
                            ins = tensor.matmul(p_s2[0:1, sl], s_ok16[:],
                                                s_q16[gg % 2][:, sl],
                                                start=(gg == 0), stop=False)
                            if h == TSL // 512 - 1:
                                ins.then_inc(fte, 1)
                    for rnd in range(2):
                        for h2 in range(TSL // 512):
                            sl2 = slice(h2 * 512, (h2 + 1) * 512)
                            tensor.matmul(p_s1[0:1, sl2], s_ok16[:], s_z16[:],
                                          start=False, stop=(rnd == 1))
                            tensor.matmul(p_s2[0:1, sl2], s_ok16[:], s_z16[:],
                                          start=False, stop=(rnd == 1))

            with nc.Block() as blk:
                @blk.vector
                def _(vector):
                    vector.tensor_scalar_mul(out=s_vec[:, 0:TSL],
                                             in0=p_s1[0:1, :], scalar1=1.0 / D)
                    vector.tensor_scalar_mul(out=s_vec[:, TSL:2 * TSL],
                                             in0=p_s2[0:1, :],
                                             scalar1=1.0 / D)
                    vector.tensor_mul(out=s_vec[:, 2 * TSL:3 * TSL],
                                      in0=s_vec[:, 0:TSL], in1=s_vec[:, 0:TSL])
                    vector.tensor_sub(out=s_vec[:, TSL:2 * TSL],
                                      in0=s_vec[:, TSL:2 * TSL],
                                      in1=s_vec[:, 2 * TSL:3 * TSL])
                    vector.tensor_scalar_add(out=s_vec[:, 2 * TSL:3 * TSL],
                                             in0=s_vec[:, TSL:2 * TSL],
                                             scalar1=EPS)
                    vector.tensor_scalar_add(out=s_vec[:, 2 * TSL:3 * TSL],
                                             in0=s_vec[:, TSL:2 * TSL],
                                             scalar1=EPS)
                    vector.tensor_scalar_mul(out=s_vec[:, 0:TSL],
                                             in0=p_s1[0:1, :], scalar1=1.0 / D)

            _sst.close()
            p_rp = _les.enter_context(
                nc.psum_tensor(f"p_rp_{idx}", [128, TSL], F32))
            p_rp2 = _les.enter_context(
                nc.psum_tensor(f"p_rp2_{idx}", [128, TSL], F32))

            with nc.Block() as blk:
                @blk.scalar
                def _(scalar):
                    scalar.activation(out=s_vec[:, TSL:2 * TSL],
                                      in_=s_vec[:, 2 * TSL:3 * TSL],
                                      func=AF.Ln)
                    scalar.activation(out=s_vec[:, TSL:2 * TSL],
                                      in_=s_vec[:, 2 * TSL:3 * TSL],
                                      func=AF.Ln)
                    scalar.activation(out=s_vec[:, 2 * TSL:3 * TSL],
                                      in_=s_vec[:, TSL:2 * TSL],
                                      func=AF.Exp, scale=-0.5)
                    scalar.activation(out=s_vec[:, 2 * TSL:3 * TSL],
                                      in_=s_vec[:, TSL:2 * TSL],
                                      func=AF.Exp, scale=-0.5).then_inc(sse, 1)

            with nc.Block() as blk:
                @blk.tensor
                def _(tensor):
                    for rnd in range(3):
                        for h in range(TSL // 512):
                            sl = slice(h * 512, (h + 1) * 512)
                            tensor.matmul(p_rp[:, sl], s_omf[:],
                                          s_vec[0:1, h * 512:(h + 1) * 512],
                                          start=True, stop=True)
                            tensor.matmul(p_rp2[:, sl], s_omf[:],
                                          s_vec[0:1, 2 * TSL + h * 512:
                                                2 * TSL + (h + 1) * 512],
                                          start=True, stop=True)

            with nc.Block() as blk:
                @blk.vector
                def _(vector):
                    vector.tensor_copy(out=s_mi[:, 0:TSL], in_=p_rp[:])
                    vector.tensor_copy(out=s_mi[:, TSL:2 * TSL], in_=p_rp2[:])
                    for gg in range(NGG):
                        if mode == "plain":
                            vector.tensor_sub(out=s_x1[gg][:], in0=s_x1[gg][:],
                                              in1=s_mi[:, 0:TSL])
                            ins = vector.tensor_mul(out=s_x1[gg][:],
                                                    in0=s_x1[gg][:],
                                                    in1=s_mi[:, TSL:2 * TSL])
                            if fuse_next is not None:
                                ins.then_inc(lnr, 1)
                                if gg >= 2:
                                    vector.wait_ge(fte, offn + gg - 1)
                                vector.tensor_copy(
                                    out=s_r16[gg % 2][:],
                                    in_=s_x1[gg][:]).then_inc(fca, 1)
                            if debug:
                                ins.then_inc(dbgs, 1)
                        elif mode == "affine":
                            vector.tensor_sub(out=s_yr[gg % 2][:],
                                              in0=s_x1[gg][:], in1=s_mi[:, 0:TSL])
                            vector.tensor_mul(out=s_yr[gg % 2][:],
                                              in0=s_yr[gg % 2][:], in1=s_mi[:, TSL:2 * TSL])
                            vector.tensor_scalar(
                                out=s_xn[gg][:], in0=s_yr[gg % 2][:],
                                scalar1=s_lnw[:, gg:gg + 1],
                                scalar2=s_lnb[:, gg:gg + 1],
                                op0=ALU.mult, op1=ALU.add)
                        else:
                            if gg >= 2:
                                vector.wait_ge(ost, (gg - 1) * 16)
                            vector.tensor_sub(out=s_x1[gg][:], in0=s_x1[gg][:],
                                              in1=s_mi[:, 0:TSL])
                            vector.tensor_mul(out=s_ost[gg % 2][:],
                                              in0=s_x1[gg][:],
                                              in1=s_mi[:, TSL:2 * TSL]).then_inc(fvv, 1)
                    if fuse_next is not None:
                        vector.tensor_copy(out=s_q16d[:],
                                           in_=s_x1[0][:, 0:512]
                                           ).then_inc(fca, 1)
                        vector.tensor_copy(out=s_q16d[:],
                                           in_=s_x1[0][:, 0:512]
                                           ).then_inc(lnr, 1)

                if fuse_next is not None:
                    p_s1n, p_s2n = fuse_next

                    @blk.scalar
                    def _(scalar):
                        for gg in range(NGG):
                            scalar.wait_ge(lnr, min(gg + 2, NGG + 1))
                            if gg >= 2:
                                scalar.wait_ge(fte, offn + gg - 1)
                            scalar.activation(
                                out=s_q16[gg % 2][:], in_=s_x1[gg][:],
                                func=AF.Square).then_inc(fsq, 1)
                        scalar.activation(out=s_q16d[:], in_=s_x1[0][:, 0:512],
                                          func=AF.Square).then_inc(fsq, 1)

                    @blk.tensor
                    def _(tensor):
                        for gg in range(NGG):
                            tensor.wait_ge(fsq, off17n + min(gg + 2, NGG + 1))
                            tensor.wait_ge(fca, off17n + min(gg + 2, NGG + 1))
                            for h in range(TSL // 512):
                                sl = slice(h * 512, (h + 1) * 512)
                                tensor.matmul(p_s1n[0:1, sl], s_ok16[:],
                                              s_r16[gg % 2][:, sl],
                                              start=(gg == 0), stop=False)
                                ins = tensor.matmul(p_s2n[0:1, sl], s_ok16[:],
                                                    s_q16[gg % 2][:, sl],
                                                    start=(gg == 0), stop=False)
                                if h == TSL // 512 - 1:
                                    ins.then_inc(fte, 1)
                        for rnd in range(2):
                            for h2 in range(TSL // 512):
                                sl2 = slice(h2 * 512, (h2 + 1) * 512)
                                tensor.matmul(p_s1n[0:1, sl2], s_ok16[:],
                                              s_z16[:], start=False,
                                              stop=(rnd == 1))
                                tensor.matmul(p_s2n[0:1, sl2], s_ok16[:],
                                              s_z16[:], start=False,
                                              stop=(rnd == 1))

                if mode == "final":
                    FVV0 = NGG + NGG * (TSL // 512)

                    @blk.sync
                    def _(sync):
                        for gg in range(NGG):
                            sync.wait_ge(fvv, FVV0 + gg + 1)
                            sync.dma_start(out=yout[gg * 128:(gg + 1) * 128, :],
                                           in_=s_ost[gg % 2][:]).then_inc(ost, 16)

                    @blk.gpsimd
                    def _(g):
                        g.wait_ge(ost, NGG * 16)

    emit_ln(0, "plain", ext_stats=(p_s1_0, p_s2_0, es0),
            fuse_next=(p_s1_1x, p_s2_1x))
    emit_ln(1, "affine", ext_stats=(p_s1_1x, p_s2_1x, es1))
    if debug:
        with nc.Block() as blk:
            @blk.gpsimd
            def _(g):
                for gg in range(NGG):
                    g.wait_ge(dbgs, gg + 1)
                    g.dma_start(out=dbg_x1[gg], in_=s_x1[gg][:]).then_inc(yld, 16)
    if debug:
        with nc.Block() as blk:
            @blk.gpsimd
            def _(g):
                dsem = nc.semaphore("dsem5").__enter__()
                for gg in range(NGG):
                    g.dma_start(out=dbg_xn[gg], in_=s_xn[gg][:]).then_inc(dsem, 16)
                g.wait_ge(dsem, NGG * 16)

    pfs = ExitStack()
    p_f = [pfs.enter_context(nc.psum_tensor(f"p_f{i}", [128, 512], F32))
           for i in range(2)]
    with nc.Block() as blk:
        @blk.tensor
        def _(tensor):
            tensor.wait_ge(wql, NGG * 16)
            for og in range(NGG):
                for th in range(TSL // 512):
                    m = og * (TSL // 512) + th
                    pt = p_f[m % 2]
                    if m >= 2:
                        tensor.wait_ge(fvv, NGG + m - 1)
                    for k in range(NGG):
                        tensor.matmul(
                            pt[:], s_Wq[k][:, og * 128:(og + 1) * 128],
                            s_xn[k][:, th * 512:(th + 1) * 512],
                            start=(k == 0), stop=False)
                    tensor.matmul(pt[:], s_Wq[0][:, 0:128], s_z16[:],
                                  start=False, stop=False)
                    tensor.matmul(pt[:], s_Wq[0][:, 0:128], s_z16[:],
                                  start=False, stop=True).then_inc(fmm, 1)

        @blk.vector
        def _(vector):
            for og in range(NGG):
                for th in range(TSL // 512):
                    m = og * (TSL // 512) + th
                    vector.wait_ge(fmm, m + 1)
                    sl = slice(th * 512, (th + 1) * 512)
                    vector.scalar_tensor_tensor(
                        out=s_x1[og][:, sl], in0=s_x1[og][:, sl], scalar=1.0,
                        in1=p_f[m % 2][:], op0=ALU.mult, op1=ALU.add,
                    ).then_inc(fvv, 1)

    if debug:
        with nc.Block() as blk:
            @blk.gpsimd
            def _(g):
                dsem = nc.semaphore("dsem6").__enter__()
                for gg in range(NGG):
                    g.dma_start(out=dbg_r2[gg], in_=s_x1[gg][:]).then_inc(dsem, 16)
                g.wait_ge(dsem, NGG * 16)

    pfs.close()
    emit_ln(2, "final")

    if debug:
        with nc.Block() as blk2:
            @blk2.gpsimd
            def _(g):
                dsem7 = nc.semaphore("dsem7").__enter__()
                g.dma_start(out=dbg_v, in_=s_vec[:]).then_inc(dsem7, 16)
                g.dma_start(out=dbg_mi, in_=s_mi[:]).then_inc(dsem7, 16)
                g.wait_ge(dsem7, 32)

    ff.close()
    ex.close()
    return nc


_NC_CACHE = None


def kernel(L0_attn_x, L0_attn_z, L0_attn_W_bc, L0_attn_W_dt, L0_attn_b_dt,
           L0_attn_A_log, L0_attn_D_param, L0_ffn_W, L0_ffn_ln_w, L0_ffn_ln_b):
    global _NC_CACHE
    import ml_dtypes
    bf = ml_dtypes.bfloat16

    x2d = np.ascontiguousarray(np.asarray(L0_attn_x, dtype=np.float32)
                               .reshape(T, D))
    W_bc = np.asarray(L0_attn_W_bc, dtype=np.float32)
    W_dt = np.asarray(L0_attn_W_dt, dtype=np.float32)
    b_dt = np.asarray(L0_attn_b_dt, dtype=np.float32)
    A_log = np.asarray(L0_attn_A_log, dtype=np.float32)
    D_param = np.asarray(L0_attn_D_param, dtype=np.float32)
    W = np.ascontiguousarray(np.asarray(L0_ffn_W, dtype=np.float32))
    ln_w = np.asarray(L0_ffn_ln_w, dtype=np.float32)
    ln_b = np.asarray(L0_ffn_ln_b, dtype=np.float32)

    ident = np.eye(128, dtype=np.float32)
    in_maps = []
    for c in range(NCORES):
        dsl = slice(c * DSL, (c + 1) * DSL)
        tsl = slice(c * TSL, (c + 1) * TSL)
        in_maps.append(dict(
            xdsT=np.ascontiguousarray(x2d[:, dsl].T),
            xtsT=np.ascontiguousarray(x2d[tsl, :].T),
            wbcT=np.ascontiguousarray(W_bc[:, dsl].T),
            wdtT=np.ascontiguousarray(W_dt[dsl, :].T),
            alog=np.ascontiguousarray(A_log[dsl, :]),
            bdt=np.ascontiguousarray(b_dt[dsl].reshape(NDG, 128).T),
            dpar=np.ascontiguousarray(D_param[dsl].reshape(NDG, 128).T),
            lnw=np.ascontiguousarray(ln_w.reshape(NGG, 128).T),
            lnb=np.ascontiguousarray(ln_b.reshape(NGG, 128).T),
            wmat=W,
            ident16=ident.astype(bf),
            ones_k16=np.ones((128, 1), dtype=bf),
            ones_kf=np.ones((128, 1), dtype=np.float32),
            ones_mf=np.ones((1, 128), dtype=np.float32),
            zer16=np.zeros((128, 512), dtype=bf),
            zerf=np.zeros((128, 1), dtype=np.float32),
        ))

    if _NC_CACHE is None:
        _NC_CACHE = build_nc()
    res = run_bass_kernel_spmd(_NC_CACHE, in_maps, core_ids=list(range(NCORES)))

    out = np.empty((1, T, D), dtype=np.float32)
    for c in range(NCORES):
        out[0, c * TSL:(c + 1) * TSL, :] = res.results[c]["yout"].T
    return out


# version 13
# speedup vs baseline: 1.0550x; 1.0550x over previous
import numpy as np
from contextlib import ExitStack

import concourse.bass as bass
import concourse.mybir as mybir
from concourse.bass_utils import run_bass_kernel_spmd
from concourse.aot_env import allow_ici_all_to_all

F32 = mybir.dt.float32
BF16 = mybir.dt.bfloat16
AF = mybir.ActivationFunctionType
ALU = mybir.AluOpType
AXX = mybir.AxisListType.X

NCORES = 8
T = 8192
D = 2048
NST = 32
RK = 64
DSL = D // NCORES
TSL = T // NCORES
NDG = DSL // 128
NGG = D // 128
TC = 2048
NTC = T // TC
NIT = NTC * NST
MAGIC = 12582912.0
EPS = 1e-5


def build_nc(debug=False):
    nc = bass.Bass("TRN2", target_bir_lowering=False)

    xdsT = nc.dram_tensor("xdsT", [DSL, T], F32, kind="ExternalInput").ap()
    xtsT = nc.dram_tensor("xtsT", [D, TSL], F32, kind="ExternalInput").ap()
    wbcT = nc.dram_tensor("wbcT", [DSL, 128], F32, kind="ExternalInput").ap()
    wdtT = nc.dram_tensor("wdtT", [RK, DSL], F32, kind="ExternalInput").ap()
    alog = nc.dram_tensor("alog", [DSL, NST], F32, kind="ExternalInput").ap()
    bdt = nc.dram_tensor("bdt", [128, NDG], F32, kind="ExternalInput").ap()
    dpar = nc.dram_tensor("dpar", [128, NDG], F32, kind="ExternalInput").ap()
    lnw = nc.dram_tensor("lnw", [128, NGG], F32, kind="ExternalInput").ap()
    lnb = nc.dram_tensor("lnb", [128, NGG], F32, kind="ExternalInput").ap()
    wmat = nc.dram_tensor("wmat", [D, D], F32, kind="ExternalInput").ap()
    ident16 = nc.dram_tensor("ident16", [128, 128], BF16, kind="ExternalInput").ap()
    ones_k16 = nc.dram_tensor("ones_k16", [128, 1], BF16, kind="ExternalInput").ap()
    ones_kf = nc.dram_tensor("ones_kf", [128, 1], F32, kind="ExternalInput").ap()
    ones_mf = nc.dram_tensor("ones_mf", [1, 128], F32, kind="ExternalInput").ap()
    zer16 = nc.dram_tensor("zer16", [128, 512], BF16, kind="ExternalInput").ap()
    zerf = nc.dram_tensor("zerf", [128, 1], F32, kind="ExternalInput").ap()

    yout = nc.dram_tensor("yout", [D, TSL], F32, kind="ExternalOutput").ap()

    ar_i = nc.dram_tensor("ar_i", [128, T], F32)
    ar_o = nc.dram_tensor("ar_o", [128, T], F32)
    a2a_i = nc.dram_tensor("a2a_i", [NCORES, NDG, 128, TSL], BF16)
    a2a_o = nc.dram_tensor("a2a_o", [NCORES, NDG, 128, TSL], BF16)
    wq_d = nc.dram_tensor("wq_d", [NGG, 128, D], BF16)
    scr_d = nc.dram_tensor("scr_d", [128, 1], F32)
    bar_i = nc.dram_tensor("bar_i", [1, 512], F32)
    bar_o = nc.dram_tensor("bar_o", [1, 512], F32)
    if debug:
        dbg_dbc = nc.dram_tensor("dbg_dbc", [128, T], F32, kind="ExternalOutput").ap()
        dbg_dl = nc.dram_tensor("dbg_dl", [NDG, 128, T], BF16, kind="ExternalOutput").ap()
        dbg_w16 = nc.dram_tensor("dbg_w16", [NDG, 128, T], BF16, kind="ExternalOutput").ap()
        dbg_yi = nc.dram_tensor("dbg_yi", [NCORES, NDG, 128, TSL], BF16, kind="ExternalOutput").ap()
        dbg_yo = nc.dram_tensor("dbg_yo", [NCORES, NDG, 128, TSL], BF16, kind="ExternalOutput").ap()
        dbg_wq = nc.dram_tensor("dbg_wq", [NGG, 128, D], BF16, kind="ExternalOutput").ap()
        dbg_x1 = nc.dram_tensor("dbg_x1", [NGG, 128, TSL], F32, kind="ExternalOutput").ap()
        dbg_xn = nc.dram_tensor("dbg_xn", [NGG, 128, TSL], BF16, kind="ExternalOutput").ap()
        dbg_r2 = nc.dram_tensor("dbg_r2", [NGG, 128, TSL], F32, kind="ExternalOutput").ap()
        dbg_yr = nc.dram_tensor("dbg_yr", [NGG, 128, TSL], F32, kind="ExternalOutput").ap()
        dbg_v = nc.dram_tensor("dbg_v", [1, 3 * TSL], F32, kind="ExternalOutput").ap()
        dbg_mi = nc.dram_tensor("dbg_mi", [128, 2 * TSL], F32, kind="ExternalOutput").ap()

    ex = ExitStack()
    E = ex.enter_context

    s_An = [E(nc.sbuf_tensor(f"s_An{g}", [128, NST], F32)) for g in range(NDG)]
    s_bdt = E(nc.sbuf_tensor("s_bdt", [128, NDG], F32))
    s_dpar = E(nc.sbuf_tensor("s_dpar", [128, NDG], F32))
    s_lnw = E(nc.sbuf_tensor("s_lnw", [128, NGG], F32))
    s_lnb = E(nc.sbuf_tensor("s_lnb", [128, NGG], F32))
    s_id16 = E(nc.sbuf_tensor("s_id16", [128, 128], BF16))
    s_ok16 = E(nc.sbuf_tensor("s_ok16", [128, 1], BF16))
    s_okf = E(nc.sbuf_tensor("s_okf", [128, 1], F32))
    s_omf = E(nc.sbuf_tensor("s_omf", [1, 128], F32))
    s_sinv = E(nc.sbuf_tensor("s_sinv", [128, 1], F32))
    s_z16 = E(nc.sbuf_tensor("s_z16", [128, 512], BF16))
    s_zf = E(nc.sbuf_tensor("s_zf", [128, 1], F32))

    sems = {}
    for name in ["ld", "xc", "wld", "qn1", "qn2", "wqs", "dbc", "ste", "sse",
                 "sve", "svf", "ars", "ccs", "repB", "repC", "dAs", "vsc",
                 "tmm", "ysv", "ya", "yld", "wql", "fca", "fsq", "fte", "fvv",
                 "fmm", "ost", "dsm", "dbgs", "wldo", "lnr",
                 "rB0", "rB1", "rB2", "rB3", "rC0", "rC1", "rC2", "rC3"]:
        sems[name] = E(nc.semaphore(name))
    ld, xc, wld = sems["ld"], sems["xc"], sems["wld"]
    qn1, qn2, wqs = sems["qn1"], sems["qn2"], sems["wqs"]
    dbc, ste, sse = sems["dbc"], sems["ste"], sems["sse"]
    sve, svf, ars = sems["sve"], sems["svf"], sems["ars"]
    ccs, repB, repC = sems["ccs"], sems["repB"], sems["repC"]
    dAs, vsc, tmm = sems["dAs"], sems["vsc"], sems["tmm"]
    ysv, ya, yld = sems["ysv"], sems["ya"], sems["yld"]
    wql, fca, fsq = sems["wql"], sems["fca"], sems["fsq"]
    fte, fvv, fmm = sems["fte"], sems["fvv"], sems["fmm"]
    ost = sems["ost"]
    dsm = sems["dsm"]
    dbgs = sems["dbgs"]
    wldo = sems["wldo"]
    lnr = sems["lnr"]
    repBs = [sems[f"rB{i}"] for i in range(4)]
    repCs = [sems[f"rC{i}"] for i in range(4)]

    ps = ExitStack()
    PS = ps.enter_context
    s_u16 = [PS(nc.sbuf_tensor(f"s_u16{g}", [128, T], BF16)) for g in range(NDG)]
    s_w16 = [PS(nc.sbuf_tensor(f"s_w16{g}", [128, T], BF16)) for g in range(NDG)]
    s_dl16 = [PS(nc.sbuf_tensor(f"s_dl16{g}", [128, T], BF16)) for g in range(NDG)]
    s_B16 = PS(nc.sbuf_tensor("s_B16", [128, TC], BF16))
    s_C16 = PS(nc.sbuf_tensor("s_C16", [128, TC], BF16))
    s_wbcT = [PS(nc.sbuf_tensor(f"s_wbcT{g}", [128, 128], BF16)) for g in range(NDG)]
    s_wdtT = PS(nc.sbuf_tensor("s_wdtT", [RK, DSL], F32))
    s_st = PS(nc.sbuf_tensor("s_st", [128, NDG * NST], F32))

    pre = ExitStack()
    PR = pre.enter_context
    s_dbcF = PR(nc.sbuf_tensor("s_dbcF", [128, T], F32))
    s_sp1 = PR(nc.sbuf_tensor("s_sp1", [128, TC], F32))
    s_sp2 = PR(nc.sbuf_tensor("s_sp2", [128, TC], F32))
    s_xld = s_sp1
    s_wt = [PR(nc.sbuf_tensor(f"s_wt{i}", [128, D], F32)) for i in range(2)]
    s_wt2 = [PR(nc.sbuf_tensor(f"s_wt2{i}", [128, D], F32)) for i in range(2)]
    s_wq = [PR(nc.sbuf_tensor(f"s_wq{i}", [128, D], BF16)) for i in range(2)]
    s_abs = PR(nc.sbuf_tensor("s_abs", [128, NGG + 3], F32))
    s_sc1 = PR(nc.sbuf_tensor("s_sc1", [1, 4], F32))
    s_row = PR(nc.sbuf_tensor("s_row", [1, 128], F32))
    p_a = PR(nc.psum_tensor("p_a", [128, 512], F32))
    p_b = PR(nc.psum_tensor("p_b", [128, 512], F32))
    p_s = PR(nc.psum_tensor("p_s", [128, 2], F32))

    N_CONST_DMA = (11 + NDG + 1) * 16
    STE0 = NGG + 4
    NCH = NDG * NTC

    _svf_n = [0]
    _svf_xb = {}
    _svf_abs = {}

    def _svf_next(kind, key):
        _svf_n[0] += 1
        if kind == "xb":
            _svf_xb[key] = _svf_n[0]
        elif kind == "abs":
            _svf_abs[key] = _svf_n[0]
        return _svf_n[0]

    _sp_order = []
    _sp_order += [("xb", (0, 0)), ("xb", (0, 1)), ("abs", (0, 0)),
                  ("xb", (0, 2)), ("xb", (0, 3)), ("abs", (0, 1))]
    for c in range(NCH - 1):
        _sp_order += [("d16", (c, 0)),
                      ("xb", (c + 1, 0)), ("xb", (c + 1, 1)), ("abs", (c + 1, 0)),
                      ("d16", (c, 1)),
                      ("xb", (c + 1, 2)), ("xb", (c + 1, 3)), ("abs", (c + 1, 1))]
    _sp_order += [("d16", (NCH - 1, 0)), ("d16", (NCH - 1, 1))]
    for kind, key in _sp_order:
        if kind != "d16":
            _svf_next(kind, key)
    _svf_n[0] = 0

    with nc.Block() as blk:
        @blk.sync
        def _(sync):
            loads = [
                (s_wdtT[:], wdtT),
                (s_bdt[:], bdt), (s_dpar[:], dpar),
                (s_lnw[:], lnw), (s_lnb[:], lnb),
                (s_id16[:], ident16), (s_ok16[:], ones_k16),
                (s_okf[:], ones_kf), (s_omf[:], ones_mf),
                (s_z16[:], zer16), (s_zf[:], zerf),
            ]
            for g in range(NDG):
                loads.append((s_An[g][:], alog[g * 128:(g + 1) * 128, :]))
            loads.append((
                s_sp2[:, 0:NDG * 128].rearrange("p (g q) -> p g q", q=128),
                wbcT.rearrange("(g p) q -> p g q", p=128)))
            for o, i in loads:
                sync.dma_start(out=o, in_=i).then_inc(ld, 16)
            for g in range(NDG):
                for q in range(NTC):
                    k = g * NTC + q
                    if k > 0:
                        sync.wait_ge(xc, k)
                    sync.dma_start(
                        out=s_xld[:],
                        in_=xdsT[g * 128:(g + 1) * 128, q * TC:(q + 1) * TC],
                    ).then_inc(ld, 16)
            for k in range(NGG):
                if k >= 2:
                    sync.wait_ge(qn1, k - 1)
                sync.dma_start(out=s_wt[k % 2][:],
                               in_=wmat[k * 128:(k + 1) * 128, :]).then_inc(wld, 16)
            for k in range(0, NGG, 2):
                j = k // 2
                if k == 0:
                    sync.wait_ge(qn1, NGG - 1)
                elif k >= 4:
                    sync.wait_ge(qn2, k - 3)
                buf = s_wt[0] if j % 2 == 0 else s_wt2[0]
                sync.dma_start(out=buf[:],
                               in_=wmat[k * 128:(k + 1) * 128, :]).then_inc(wld, 16)
            sync.wait_ge(qn2, NGG)
            for k in range(2):
                sync.dma_start(out=s_wt[k][:],
                               in_=wmat[k * 128:(k + 1) * 128, :]).then_inc(wld, 16)

        @blk.gpsimd
        def _(g):
            g.wait_ge(dbc, NGG)
            g.dma_start(out=ar_i.ap(), in_=s_dbcF[:]).then_inc(ars, 16)
            g.wait_ge(ars, 16)
            g.collective_compute(
                "AllReduce", ALU.add,
                replica_groups=[list(range(NCORES))],
                ins=[ar_i.ap().opt()], outs=[ar_o.ap().opt()],
            ).then_inc(ccs, 1)
            g.wait_ge(ccs, 1)
            g.dma_start(out=s_dbcF[:], in_=ar_o.ap()).then_inc(ars, 16)
            for k in range(NGG):
                g.wait_ge(qn2, k + 1)
                g.dma_start(out=wq_d.ap()[k], in_=s_wq[k % 2][:]).then_inc(wqs, 16)
            for k in range(2):
                g.wait_ge(qn2, NGG + k + 1)
                g.dma_start(out=wq_d.ap()[k], in_=s_wq[k][:]).then_inc(wqs, 16)

        @blk.vector
        def _(vector):
            vector.wait_ge(ld, N_CONST_DMA)
            for g in range(NDG):
                for q in range(NTC):
                    k = g * NTC + q
                    vector.wait_ge(ld, N_CONST_DMA + (k + 1) * 16)
                    vector.tensor_copy(
                        out=s_u16[g][:, q * TC:(q + 1) * TC], in_=s_xld[:]
                    ).then_inc(xc, 1)
            for g in range(NDG):
                vector.tensor_copy(out=s_wbcT[g][:],
                                   in_=s_sp2[:, g * 128:(g + 1) * 128])
            vector.wait_ge(sse, 1)
            for g in range(NDG):
                ins = vector.tensor_scalar_mul(out=s_An[g][:], in0=s_An[g][:],
                                               scalar1=-1.0)
            ins.then_inc(sve, 1)

            for tt in range(NGG):
                vector.wait_ge(ste, tt + 1)
                pt = p_a if tt % 2 == 0 else p_b
                vector.tensor_copy(out=s_dbcF[:, tt * 512:(tt + 1) * 512],
                                   in_=pt[:]).then_inc(dbc, 1)

            for k in range(NGG):
                vector.wait_ge(wld, (k + 1) * 16)
                vector.tensor_reduce(
                    out=s_abs[:, k:k + 1], in_=s_wt[k % 2][:], axis=AXX,
                    op=ALU.add, apply_absolute_value=True,
                ).then_inc(qn1, 1)
            vector.tensor_reduce(
                out=s_abs[:, NGG + 1:NGG + 2], in_=s_wt[(NGG - 1) % 2][:],
                axis=AXX, op=ALU.add, apply_absolute_value=True)
            vector.reduce_sum(out=s_abs[:, NGG:NGG + 1], in_=s_abs[:, 0:NGG],
                              axis=AXX)
            vector.reduce_sum(out=s_abs[:, NGG:NGG + 1], in_=s_abs[:, 0:NGG],
                              axis=AXX).then_inc(sve, 1)
            vector.wait_ge(ste, NGG + 2)
            vector.tensor_scalar_mul(out=s_sc1[:, 0:1], in0=p_s[0:1, 0:1],
                                     scalar1=1.0 / (D * D))
            vector.tensor_scalar_mul(out=s_sc1[:, 0:1], in0=p_s[0:1, 0:1],
                                     scalar1=1.0 / (D * D))
            vector.tensor_scalar_mul(out=s_sc1[:, 0:1], in0=p_s[0:1, 0:1],
                                     scalar1=1.0 / (D * D)).then_inc(sve, 1)
            vector.wait_ge(ste, NGG + 4)
            vector.tensor_copy(out=s_sinv[:], in_=p_s[:, 0:1])
            vector.tensor_copy(out=s_sinv[:], in_=p_s[:, 0:1])
            vector.tensor_copy(out=s_sinv[:], in_=p_s[:, 0:1])
            vector.wait_ge(ars, 32)
            for tcq in range(NTC):
                vector.tensor_copy(out=s_B16[tcq * NST:(tcq + 1) * NST, :],
                                   in_=s_dbcF[64:96, tcq * TC:(tcq + 1) * TC])
                vector.tensor_copy(out=s_C16[tcq * NST:(tcq + 1) * NST, :],
                                   in_=s_dbcF[96:128, tcq * TC:(tcq + 1) * TC])
            for kind, key in _sp_order:
                if kind == "xb":
                    c, j = key
                    g = c // NTC
                    m = 4 * c + j
                    vector.wait_ge(ste, STE0 + m + 1)
                    pt = p_a if m % 2 == 0 else p_b
                    vector.tensor_scalar(
                        out=s_sp1[:, j * 512:(j + 1) * 512], in0=pt[:],
                        scalar1=s_bdt[:, g:g + 1], scalar2=None,
                        op0=ALU.add).then_inc(svf, 1)
                elif kind == "abs":
                    c, h = key
                    hs = slice(h * 1024, (h + 1) * 1024)
                    vector.scalar_tensor_tensor(
                        out=s_sp2[:, hs], in0=s_sp1[:, hs], scalar=-1.0,
                        in1=s_sp1[:, hs],
                        op0=ALU.mult, op1=ALU.min).then_inc(svf, 1)
                else:
                    c, h = key
                    g, q = divmod(c, NTC)
                    hs = slice(h * 1024, (h + 1) * 1024)
                    vector.wait_ge(sse, 2 + 2 * c + h + 1)
                    vector.scalar_tensor_tensor(
                        out=s_dl16[g][:, q * TC + h * 1024:q * TC + (h + 1) * 1024],
                        in0=s_sp1[:, hs], scalar=0.0, in1=s_sp2[:, hs],
                        op0=ALU.max, op1=ALU.add)
            for g in range(NDG):
                vector.tensor_mul(out=s_w16[g][:], in0=s_dl16[g][:],
                                  in1=s_u16[g][:])
            for k in range(NGG):
                j = k // 2
                if k % 2 == 0:
                    vector.wait_ge(wld, (NGG + j + 1) * 16)
                    wbuf = s_wt[0] if j % 2 == 0 else s_wt2[0]
                else:
                    vector.wait_ge(wldo, (j + 1) * 16)
                    wbuf = s_wt[1] if j % 2 == 0 else s_wt2[1]
                if k >= 2:
                    vector.wait_ge(wqs, (k - 1) * 16)
                vector.tensor_scalar(out=s_sp1[:, 0:D], in0=wbuf[:],
                                     scalar1=s_sinv[:], scalar2=MAGIC,
                                     op0=ALU.mult, op1=ALU.add)
                vector.tensor_scalar(out=s_sp1[:, 0:D], in0=s_sp1[:, 0:D],
                                     scalar1=MAGIC, scalar2=1.0,
                                     op0=ALU.subtract, op1=ALU.min)
                vector.tensor_scalar_max(out=s_wq[k % 2][:], in0=s_sp1[:, 0:D],
                                         scalar1=-1.0).then_inc(qn2, 1)
            for k in range(2):
                vector.wait_ge(wld, (NGG + NGG // 2 + k + 1) * 16)
                vector.wait_ge(wqs, NGG * 16)
                vector.tensor_scalar(out=s_sp1[:, 0:D], in0=s_wt[k][:],
                                     scalar1=s_sinv[:], scalar2=MAGIC,
                                     op0=ALU.mult, op1=ALU.add)
                vector.tensor_scalar(out=s_sp1[:, 0:D], in0=s_sp1[:, 0:D],
                                     scalar1=MAGIC, scalar2=1.0,
                                     op0=ALU.subtract, op1=ALU.min)
                vector.tensor_scalar_max(out=s_wq[k][:], in0=s_sp1[:, 0:D],
                                         scalar1=-1.0).then_inc(qn2, 1)

        @blk.scalar
        def _(scalar):
            scalar.wait_ge(ld, N_CONST_DMA)
            for g in range(NDG):
                ins = scalar.activation(out=s_An[g][:], in_=s_An[g][:],
                                        func=AF.Exp)
            ins.then_inc(sse, 1)
            scalar.wait_ge(sve, 3)
            scalar.activation(out=s_sc1[:, 1:2], in_=s_sc1[:, 0:1], func=AF.Ln)
            scalar.activation(out=s_sc1[:, 1:2], in_=s_sc1[:, 0:1], func=AF.Ln)
            scalar.activation(out=s_sc1[:, 2:3], in_=s_sc1[:, 1:2],
                              func=AF.Exp, scale=-1.0)
            scalar.activation(out=s_sc1[:, 2:3], in_=s_sc1[:, 1:2],
                              func=AF.Exp, scale=-1.0).then_inc(sse, 1)
            for c in range(NCH):
                for h in range(2):
                    hs = slice(h * 1024, (h + 1) * 1024)
                    scalar.wait_ge(svf, _svf_abs[(c, h)])
                    scalar.activation(out=s_sp2[:, hs], in_=s_sp2[:, hs],
                                      func=AF.Exp)
                    scalar.activation(out=s_sp2[:, hs], in_=s_sp2[:, hs],
                                      func=AF.Ln, bias=1.0).then_inc(sse, 1)
            for k in range(1, NGG, 2):
                j = k // 2
                if k == 1:
                    scalar.wait_ge(qn1, NGG)
                elif k >= 5:
                    scalar.wait_ge(qn2, k - 3)
                buf = s_wt[1] if j % 2 == 0 else s_wt2[1]
                scalar.dma_start(out=buf[:],
                                 in_=wmat[k * 128:(k + 1) * 128, :]
                                 ).then_inc(wldo, 16)

        @blk.tensor
        def _(tensor):
            tensor.wait_ge(sve, 1)
            for tt in range(NGG):
                pt = p_a if tt % 2 == 0 else p_b
                if tt >= 2:
                    tensor.wait_ge(dbc, tt - 1)
                for g in range(NDG):
                    tensor.matmul(pt[:], s_wbcT[g][:],
                                  s_u16[g][:, tt * 512:(tt + 1) * 512],
                                  start=(g == 0), stop=False)
                tensor.matmul(pt[:], s_wbcT[0][:], s_z16[:],
                              start=False, stop=True).then_inc(ste, 1)
            tensor.wait_ge(sve, 2)
            tensor.matmul(p_s[0:1, 0:1], s_okf[:], s_abs[:, NGG:NGG + 1],
                          start=True, stop=False)
            tensor.matmul(p_s[0:1, 0:1], s_okf[:], s_zf[:, 0:1],
                          start=False, stop=False)
            tensor.matmul(p_s[0:1, 0:1], s_okf[:], s_zf[:, 0:1],
                          start=False, stop=True).then_inc(ste, 2)
            tensor.wait_ge(sse, 2)
            tensor.matmul(p_s[:, 0:1], s_omf[:], s_sc1[0:1, 2:3],
                          start=True, stop=False)
            tensor.matmul(p_s[:, 0:1], s_omf[:], s_zf[0:1, 0:1],
                          start=False, stop=False)
            tensor.matmul(p_s[:, 0:1], s_omf[:], s_zf[0:1, 0:1],
                          start=False, stop=True).then_inc(ste, 2)
            tensor.wait_ge(ars, 32)
            for m in range(NDG * NGG):
                g, tt4 = divmod(m, NGG)
                pt = p_a if m % 2 == 0 else p_b
                if m >= 2:
                    mm = m - 2
                    tensor.wait_ge(svf, _svf_xb[(mm // 4, mm % 4)])
                tensor.matmul(pt[:], s_wdtT[:, g * 128:(g + 1) * 128],
                              s_dbcF[0:RK, tt4 * 512:(tt4 + 1) * 512],
                              start=True, stop=True).then_inc(ste, 1)


    if debug:
        with nc.Block() as blk:
            @blk.gpsimd
            def _(g):
                dsem = nc.semaphore("dsem1").__enter__()
                g.dma_start(out=dbg_dbc, in_=s_dbcF[:]).then_inc(dsem, 16)
                for gg in range(NDG):
                    g.dma_start(out=dbg_dl[gg], in_=s_dl16[gg][:]).then_inc(dsem, 16)
                    g.dma_start(out=dbg_w16[gg], in_=s_w16[gg][:]).then_inc(dsem, 16)
                g.wait_ge(dsem, (1 + 2 * NDG) * 16)

    pre.close()

    sc = ExitStack()
    SC = sc.enter_context
    s_Brep = [SC(nc.sbuf_tensor(f"s_Brep{i}", [128, TC], BF16)) for i in range(4)]
    s_Crep = [SC(nc.sbuf_tensor(f"s_Crep{i}", [128, TC], BF16)) for i in range(4)]
    s_dA = [SC(nc.sbuf_tensor(f"s_dA{g}", [128, TC], F32)) for g in range(NDG)]
    s_wb = SC(nc.sbuf_tensor("s_wb", [128, TC], BF16))
    s_h = SC(nc.sbuf_tensor("s_h", [128, TC], BF16))
    s_hc = [SC(nc.sbuf_tensor(f"s_hc{i}", [128, TC], BF16)) for i in range(2)]
    s_ys = [SC(nc.sbuf_tensor(f"s_ys{g}", [128, TC], BF16)) for g in range(NDG)]
    p_y = [SC(nc.psum_tensor(f"p_y{g}", [128, TC], F32)) for g in range(NDG)]

    def chains_interleaved(eng, bufs, rows, sem, gate, it_range=None, state=None,
                           group=2):
        depth = len(bufs)
        steps = [(0, 1)]
        k = 1
        while k < 128:
            c = min(k, 128 - k)
            steps.append((k, c))
            k += c
        nst = len(steps)
        if state is None:
            state = [0]
        if it_range is None:
            it_range = range(0, NIT)
        lo, hi = it_range.start, it_range.stop
        for itg in range(lo, hi, group):
            grp = [itg + j for j in range(group) if itg + j < hi]
            for s in range(nst):
                for it in grp:
                    slot = it % depth
                    base = (it // depth) * nst * 16
                    if s == 0:
                        gate(eng, it)
                        eng.dma_start(out=bufs[slot][0:1, :],
                                      in_=rows[it]).then_inc(sem[slot], 16)
                    else:
                        k0, c = steps[s]
                        eng.wait_ge(sem[slot], base + s * 16)
                        eng.dma_start(out=bufs[slot][k0:k0 + c, :],
                                      in_=bufs[slot][0:c, :]
                                      ).then_inc(sem[slot], 16)

    with nc.Block() as blk:
        @blk.sync
        def _(sync):
            rowsB = []
            for it in range(NIT):
                tc, n = divmod(it, NST)
                rowsB.append(s_B16[tc * NST + n:tc * NST + n + 1, :])

            def gateB(eng, it):
                if it >= 4:
                    eng.wait_ge(vsc, (it - 3) * NDG)
            chains_interleaved(sync, s_Brep, rowsB, repBs, gateB)

        @blk.gpsimd
        def _(g):
            rowsC = []
            for it in range(NIT):
                tc, n = divmod(it, NST)
                rowsC.append(s_C16[tc * NST + n:tc * NST + n + 1, :])

            def gateC(eng, it):
                if it >= 4:
                    eng.wait_ge(vsc, (it - 3) * NDG)
            for tc in range(NTC):
                chains_interleaved(g, s_Crep, rowsC, repCs, gateC,
                                   it_range=range(tc * NST, (tc + 1) * NST))
                for dg in range(NDG):
                    g.wait_ge(ysv, tc * NDG + dg + 1)
                    for h in range(TC // TSL):
                        dest = tc * (TC // TSL) + h
                        g.dma_start(
                            out=a2a_i.ap()[dest, dg],
                            in_=s_ys[dg][:, h * TSL:(h + 1) * TSL],
                        ).then_inc(ya, 16)

        @blk.scalar
        def _(scalar):
            for it in range(NIT):
                tc, n = divmod(it, NST)
                for g in range(NDG):
                    idx = it * NDG + g
                    if idx >= NDG:
                        scalar.wait_ge(vsc, idx - NDG + 1)
                    scalar.activation(
                        out=s_dA[g][:], in_=s_dl16[g][:, tc * TC:(tc + 1) * TC],
                        func=AF.Exp, scale=s_An[g][:, n:n + 1],
                    ).then_inc(dAs, 1)

        @blk.vector
        def _(vector):
            for tc in range(NTC):
                for n in range(NST):
                    it = tc * NST + n
                    vector.wait_ge(repBs[it % 4], 128 * (it // 4 + 1))
                    vector.wait_ge(repCs[it % 4], 128 * (it // 4 + 1))
                    for g in range(NDG):
                        idx = it * NDG + g
                        col = g * NST + n
                        vector.wait_ge(dAs, idx + 1)
                        vector.tensor_mul(out=s_wb[:],
                                          in0=s_w16[g][:, tc * TC:(tc + 1) * TC],
                                          in1=s_Brep[it % 4][:])
                        init = 0.0 if tc == 0 else s_st[:, col:col + 1]
                        vector.tensor_tensor_scan(
                            out=s_h[:], data0=s_dA[g][:], data1=s_wb[:],
                            initial=init, op0=ALU.mult, op1=ALU.add)
                        if idx >= 2:
                            vector.wait_ge(tmm, idx - 1)
                        vector.tensor_mul(out=s_hc[idx % 2][:], in0=s_h[:],
                                          in1=s_Crep[it % 4][:]).then_inc(vsc, 1)
                        vector.tensor_copy(out=s_st[:, col:col + 1],
                                           in_=s_h[:, TC - 1:TC])
                for g in range(NDG):
                    vector.wait_ge(tmm, (tc * NST + NST - 1) * NDG + g + 1)
                    if tc >= 1:
                        vector.wait_ge(ya, (tc - 1) * NDG * 32 + (g + 1) * 32)
                    vector.scalar_tensor_tensor(
                        out=s_ys[g][:], in0=s_u16[g][:, tc * TC:(tc + 1) * TC],
                        scalar=s_dpar[:, g:g + 1], in1=p_y[g][:],
                        op0=ALU.mult, op1=ALU.add).then_inc(ysv, 1)

        @blk.tensor
        def _(tensor):
            for it in range(NIT):
                tc, n = divmod(it, NST)
                for g in range(NDG):
                    idx = it * NDG + g
                    tensor.wait_ge(vsc, idx + 1)
                    if n == 0 and tc >= 1:
                        tensor.wait_ge(ysv, (tc - 1) * NDG + g + 1)
                    for j in range(TC // 512):
                        tensor.matmul(
                            p_y[g][:, j * 512:(j + 1) * 512], s_id16[:],
                            s_hc[idx % 2][:, j * 512:(j + 1) * 512],
                            start=(n == 0), stop=False)
                    if n == NST - 1:
                        for j in range(TC // 512):
                            ins = tensor.matmul(
                                p_y[g][:, j * 512:(j + 1) * 512], s_id16[:],
                                s_z16[:], start=False, stop=True)
                        ins.then_inc(tmm, 1)
                    else:
                        ins = tensor.matmul(
                            p_y[g][:, 0:512], s_id16[:], s_z16[:],
                            start=False, stop=False)
                        ins.then_inc(tmm, 1)

    if debug:
        with nc.Block() as blk:
            @blk.gpsimd
            def _(g):
                dsem = nc.semaphore("dsem2").__enter__()
                g.dma_start(out=dbg_yi, in_=a2a_i.ap()).then_inc(dsem, 16)
                g.dma_start(out=dbg_wq, in_=wq_d.ap()).then_inc(dsem, 16)
                g.wait_ge(dsem, 32)

    sc.close()
    ps.close()

    ff = ExitStack()
    FF = ff.enter_context
    s_x1 = [FF(nc.sbuf_tensor(f"s_x1{g}", [128, TSL], F32)) for g in range(NGG)]
    s_Wq = [FF(nc.sbuf_tensor(f"s_Wq{g}", [128, D], BF16)) for g in range(NGG)]
    s_xn = [FF(nc.sbuf_tensor(f"s_xn{g}", [128, TSL], BF16)) for g in range(NGG)]
    s_yr = [FF(nc.sbuf_tensor(f"s_yr{i}", [128, TSL], BF16)) for i in range(2)]
    s_r16 = [FF(nc.sbuf_tensor(f"s_r16{i}", [128, TSL], BF16)) for i in range(3)]
    s_q16 = [FF(nc.sbuf_tensor(f"s_q16{i}", [128, TSL], BF16)) for i in range(3)]
    s_q16d = FF(nc.sbuf_tensor("s_q16d", [128, 512], BF16))
    s_mi = FF(nc.sbuf_tensor("s_mi", [128, 2 * TSL], F32))
    s_vec = FF(nc.sbuf_tensor("s_vec", [1, 3 * TSL], F32))
    s_ost = [FF(nc.sbuf_tensor(f"s_ost{i}", [128, TSL], F32)) for i in range(2)]

    LD0 = N_CONST_DMA + NDG * NTC * 16

    es1 = ExitStack()
    p_s1_1x = es1.enter_context(nc.psum_tensor("p_s1_1x", [1, TSL], F32))
    p_s2_1x = es1.enter_context(nc.psum_tensor("p_s2_1x", [1, TSL], F32))
    es0 = ExitStack()
    p_s1_0 = es0.enter_context(nc.psum_tensor("p_s1_0", [1, TSL], F32))
    p_s2_0 = es0.enter_context(nc.psum_tensor("p_s2_0", [1, TSL], F32))
    with nc.Block() as blk:
        @blk.gpsimd
        def _(g):
            g.wait_ge(ya, NTC * NDG * 2 * 16)
            with allow_ici_all_to_all():
                g.collective_compute(
                    "AllToAll", ALU.bypass,
                    replica_groups=[list(range(NCORES))],
                    ins=[a2a_i.ap().opt()], outs=[a2a_o.ap().opt()],
                ).then_inc(ccs, 1)
            g.wait_ge(ccs, 2)
            g.collective_compute(
                "AllReduce", ALU.add,
                replica_groups=[list(range(NCORES))],
                ins=[bar_i.ap().opt()],
                outs=[bar_o.ap().opt()],
            ).then_inc(ccs, 1)
            g.wait_ge(ccs, 3)
            for gg in range(NGG):
                src, dg = divmod(gg, NDG)
                if gg >= 2:
                    g.wait_ge(fvv, gg - 1)
                g.dma_start(out=s_yr[gg % 2][:],
                            in_=a2a_o.ap()[src, dg]).then_inc(yld, 16)
            g.dma_start(out=s_ost[0][:], in_=a2a_o.ap()[0, 0]).then_inc(yld, 16)

        @blk.sync
        def _(sync):
            for gg in range(NGG):
                sync.dma_start(out=s_x1[gg][:],
                               in_=xtsT[gg * 128:(gg + 1) * 128, :]).then_inc(ld, 16)
            sync.dma_start(out=s_ost[1][:],
                           in_=xtsT[0:128, :]).then_inc(ld, 16)
            sync.wait_ge(wqs, (NGG + 2) * 16)
            sync.wait_ge(fvv, NGG)
            for gg in range(NGG):
                sync.dma_start(out=s_Wq[gg][:], in_=wq_d.ap()[gg]).then_inc(wql, 16)

        @blk.vector
        def _(vector):
            for gg in range(NGG):
                vector.wait_ge(ld, LD0 + min(gg + 2, NGG + 1) * 16)
                vector.wait_ge(yld, min(gg + 2, NGG + 1) * 16)
                vector.tensor_add(out=s_x1[gg][:], in0=s_x1[gg][:],
                                  in1=s_yr[gg % 2][:]).then_inc(fvv, 1)
                if gg >= 3:
                    vector.wait_ge(fte, gg - 2)
                vector.tensor_copy(out=s_r16[gg % 3][:],
                                   in_=s_x1[gg][:]).then_inc(fca, 1)
            vector.tensor_copy(out=s_q16d[:],
                               in_=s_x1[0][:, 0:512]).then_inc(fca, 1)

        @blk.scalar
        def _(scalar):
            for gg in range(NGG):
                scalar.wait_ge(fvv, gg + 1)
                if gg >= 3:
                    scalar.wait_ge(fte, gg - 2)
                scalar.activation(out=s_q16[gg % 3][:], in_=s_x1[gg][:],
                                  func=AF.Square).then_inc(fsq, 1)
            scalar.activation(out=s_q16d[:], in_=s_x1[0][:, 0:512],
                              func=AF.Square).then_inc(fsq, 1)

        @blk.tensor
        def _(tensor):
            for gg in range(NGG):
                tensor.wait_ge(fsq, min(gg + 2, NGG + 1))
                tensor.wait_ge(fca, min(gg + 2, NGG + 1))
                for h in range(TSL // 512):
                    sl = slice(h * 512, (h + 1) * 512)
                    tensor.matmul(p_s1_0[0:1, sl], s_ok16[:],
                                  s_r16[gg % 3][:, sl],
                                  start=(gg == 0), stop=False)
                    ins = tensor.matmul(p_s2_0[0:1, sl], s_ok16[:],
                                        s_q16[gg % 3][:, sl],
                                        start=(gg == 0), stop=False)
                    if h == TSL // 512 - 1:
                        ins.then_inc(fte, 1)
            for rnd in range(2):
                for h2 in range(TSL // 512):
                    sl2 = slice(h2 * 512, (h2 + 1) * 512)
                    tensor.matmul(p_s1_0[0:1, sl2], s_ok16[:], s_z16[:],
                                  start=False, stop=(rnd == 1))
                    tensor.matmul(p_s2_0[0:1, sl2], s_ok16[:], s_z16[:],
                                  start=False, stop=(rnd == 1))
        if debug:
            @blk.gpsimd
            def _(g):
                g.wait_ge(fvv, NGG)
                for gg in range(NGG):
                    g.dma_start(out=dbg_yr[gg], in_=s_x1[gg][:]).then_inc(yld, 16)
    if debug:
        with nc.Block() as blk:
            @blk.gpsimd
            def _(g):
                dsem = nc.semaphore("dsem3").__enter__()
                g.dma_start(out=dbg_yo, in_=a2a_o.ap()).then_inc(dsem, 16)
                g.wait_ge(dsem, 16)

    def emit_ln(idx, mode, ext_stats=None, fuse_next=None):
        off = idx * NGG
        off17 = idx * (NGG + 1)
        offn = (idx + 1) * NGG
        off17n = (idx + 1) * (NGG + 1)
        with ExitStack() as _les:
            if ext_stats is not None:
                p_s1, p_s2, _sst = ext_stats
            else:
                _sst = ExitStack()
                p_s1 = _sst.enter_context(
                    nc.psum_tensor(f"p_s1_{idx}", [1, TSL], F32))
                p_s2 = _sst.enter_context(
                    nc.psum_tensor(f"p_s2_{idx}", [1, TSL], F32))
            if ext_stats is None:
              with nc.Block() as blk:
                @blk.scalar
                def _(scalar):
                    for gg in range(NGG):
                        if gg >= 3:
                            scalar.wait_ge(fte, off + gg - 2)
                        scalar.activation(out=s_q16[gg % 3][:], in_=s_x1[gg][:],
                                          func=AF.Square).then_inc(fsq, 1)
                    scalar.activation(out=s_q16d[:], in_=s_x1[0][:, 0:512],
                                      func=AF.Square).then_inc(fsq, 1)

                @blk.vector
                def _(vector):
                    for gg in range(NGG):
                        if gg >= 3:
                            vector.wait_ge(fte, off + gg - 2)
                        vector.tensor_copy(out=s_r16[gg % 3][:],
                                           in_=s_x1[gg][:]).then_inc(fca, 1)
                    vector.tensor_copy(out=s_q16d[:],
                                       in_=s_x1[0][:, 0:512]).then_inc(fca, 1)

                @blk.tensor
                def _(tensor):
                    for gg in range(NGG):
                        tensor.wait_ge(fsq, off17 + min(gg + 2, NGG + 1))
                        tensor.wait_ge(fca, off17 + min(gg + 2, NGG + 1))
                        for h in range(TSL // 512):
                            sl = slice(h * 512, (h + 1) * 512)
                            tensor.matmul(p_s1[0:1, sl], s_ok16[:],
                                          s_r16[gg % 3][:, sl],
                                          start=(gg == 0), stop=False)
                            ins = tensor.matmul(p_s2[0:1, sl], s_ok16[:],
                                                s_q16[gg % 3][:, sl],
                                                start=(gg == 0), stop=False)
                            if h == TSL // 512 - 1:
                                ins.then_inc(fte, 1)
                    for rnd in range(2):
                        for h2 in range(TSL // 512):
                            sl2 = slice(h2 * 512, (h2 + 1) * 512)
                            tensor.matmul(p_s1[0:1, sl2], s_ok16[:], s_z16[:],
                                          start=False, stop=(rnd == 1))
                            tensor.matmul(p_s2[0:1, sl2], s_ok16[:], s_z16[:],
                                          start=False, stop=(rnd == 1))

            with nc.Block() as blk:
                @blk.vector
                def _(vector):
                    vector.tensor_scalar_mul(out=s_vec[:, 0:TSL],
                                             in0=p_s1[0:1, :], scalar1=1.0 / D)
                    vector.tensor_scalar_mul(out=s_vec[:, TSL:2 * TSL],
                                             in0=p_s2[0:1, :],
                                             scalar1=1.0 / D)
                    vector.tensor_mul(out=s_vec[:, 2 * TSL:3 * TSL],
                                      in0=s_vec[:, 0:TSL], in1=s_vec[:, 0:TSL])
                    vector.tensor_sub(out=s_vec[:, TSL:2 * TSL],
                                      in0=s_vec[:, TSL:2 * TSL],
                                      in1=s_vec[:, 2 * TSL:3 * TSL])
                    vector.tensor_scalar_add(out=s_vec[:, 2 * TSL:3 * TSL],
                                             in0=s_vec[:, TSL:2 * TSL],
                                             scalar1=EPS)
                    vector.tensor_scalar_add(out=s_vec[:, 2 * TSL:3 * TSL],
                                             in0=s_vec[:, TSL:2 * TSL],
                                             scalar1=EPS)
                    vector.tensor_scalar_mul(out=s_vec[:, 0:TSL],
                                             in0=p_s1[0:1, :], scalar1=1.0 / D)

            _sst.close()
            p_rp = _les.enter_context(
                nc.psum_tensor(f"p_rp_{idx}", [128, TSL], F32))
            p_rp2 = _les.enter_context(
                nc.psum_tensor(f"p_rp2_{idx}", [128, TSL], F32))

            with nc.Block() as blk:
                @blk.scalar
                def _(scalar):
                    scalar.activation(out=s_vec[:, TSL:2 * TSL],
                                      in_=s_vec[:, 2 * TSL:3 * TSL],
                                      func=AF.Ln)
                    scalar.activation(out=s_vec[:, TSL:2 * TSL],
                                      in_=s_vec[:, 2 * TSL:3 * TSL],
                                      func=AF.Ln)
                    scalar.activation(out=s_vec[:, 2 * TSL:3 * TSL],
                                      in_=s_vec[:, TSL:2 * TSL],
                                      func=AF.Exp, scale=-0.5)
                    scalar.activation(out=s_vec[:, 2 * TSL:3 * TSL],
                                      in_=s_vec[:, TSL:2 * TSL],
                                      func=AF.Exp, scale=-0.5).then_inc(sse, 1)

            with nc.Block() as blk:
                @blk.tensor
                def _(tensor):
                    for rnd in range(3):
                        for h in range(TSL // 512):
                            sl = slice(h * 512, (h + 1) * 512)
                            tensor.matmul(p_rp[:, sl], s_omf[:],
                                          s_vec[0:1, h * 512:(h + 1) * 512],
                                          start=True, stop=True)
                            tensor.matmul(p_rp2[:, sl], s_omf[:],
                                          s_vec[0:1, 2 * TSL + h * 512:
                                                2 * TSL + (h + 1) * 512],
                                          start=True, stop=True)

            with nc.Block() as blk:
                @blk.vector
                def _(vector):
                    vector.tensor_copy(out=s_mi[:, 0:TSL], in_=p_rp[:])
                    vector.tensor_copy(out=s_mi[:, TSL:2 * TSL], in_=p_rp2[:])
                    for gg in range(NGG):
                        if mode == "plain":
                            vector.tensor_sub(out=s_x1[gg][:], in0=s_x1[gg][:],
                                              in1=s_mi[:, 0:TSL])
                            ins = vector.tensor_mul(out=s_x1[gg][:],
                                                    in0=s_x1[gg][:],
                                                    in1=s_mi[:, TSL:2 * TSL])
                            if fuse_next is not None:
                                ins.then_inc(lnr, 1)
                                if gg >= 3:
                                    vector.wait_ge(fte, offn + gg - 2)
                                vector.tensor_copy(
                                    out=s_r16[gg % 3][:],
                                    in_=s_x1[gg][:]).then_inc(fca, 1)
                            if debug:
                                ins.then_inc(dbgs, 1)
                        elif mode == "affine":
                            vector.tensor_sub(out=s_yr[gg % 2][:],
                                              in0=s_x1[gg][:], in1=s_mi[:, 0:TSL])
                            vector.tensor_mul(out=s_yr[gg % 2][:],
                                              in0=s_yr[gg % 2][:], in1=s_mi[:, TSL:2 * TSL])
                            vector.tensor_scalar(
                                out=s_xn[gg][:], in0=s_yr[gg % 2][:],
                                scalar1=s_lnw[:, gg:gg + 1],
                                scalar2=s_lnb[:, gg:gg + 1],
                                op0=ALU.mult, op1=ALU.add)
                        else:
                            if gg >= 2:
                                vector.wait_ge(ost, (gg - 1) * 16)
                            vector.tensor_sub(out=s_x1[gg][:], in0=s_x1[gg][:],
                                              in1=s_mi[:, 0:TSL])
                            vector.tensor_mul(out=s_ost[gg % 2][:],
                                              in0=s_x1[gg][:],
                                              in1=s_mi[:, TSL:2 * TSL]).then_inc(fvv, 1)
                    if fuse_next is not None:
                        vector.tensor_copy(out=s_q16d[:],
                                           in_=s_x1[0][:, 0:512]
                                           ).then_inc(fca, 1)
                        vector.tensor_copy(out=s_q16d[:],
                                           in_=s_x1[0][:, 0:512]
                                           ).then_inc(lnr, 1)

                if fuse_next is not None:
                    p_s1n, p_s2n = fuse_next

                    @blk.scalar
                    def _(scalar):
                        for gg in range(NGG):
                            scalar.wait_ge(lnr, min(gg + 2, NGG + 1))
                            if gg >= 3:
                                scalar.wait_ge(fte, offn + gg - 2)
                            scalar.activation(
                                out=s_q16[gg % 3][:], in_=s_x1[gg][:],
                                func=AF.Square).then_inc(fsq, 1)
                        scalar.activation(out=s_q16d[:], in_=s_x1[0][:, 0:512],
                                          func=AF.Square).then_inc(fsq, 1)

                    @blk.tensor
                    def _(tensor):
                        for gg in range(NGG):
                            tensor.wait_ge(fsq, off17n + min(gg + 2, NGG + 1))
                            tensor.wait_ge(fca, off17n + min(gg + 2, NGG + 1))
                            for h in range(TSL // 512):
                                sl = slice(h * 512, (h + 1) * 512)
                                tensor.matmul(p_s1n[0:1, sl], s_ok16[:],
                                              s_r16[gg % 3][:, sl],
                                              start=(gg == 0), stop=False)
                                ins = tensor.matmul(p_s2n[0:1, sl], s_ok16[:],
                                                    s_q16[gg % 3][:, sl],
                                                    start=(gg == 0), stop=False)
                                if h == TSL // 512 - 1:
                                    ins.then_inc(fte, 1)
                        for rnd in range(2):
                            for h2 in range(TSL // 512):
                                sl2 = slice(h2 * 512, (h2 + 1) * 512)
                                tensor.matmul(p_s1n[0:1, sl2], s_ok16[:],
                                              s_z16[:], start=False,
                                              stop=(rnd == 1))
                                tensor.matmul(p_s2n[0:1, sl2], s_ok16[:],
                                              s_z16[:], start=False,
                                              stop=(rnd == 1))

                if mode == "final":
                    FVV0 = NGG + NGG * (TSL // 512)

                    @blk.sync
                    def _(sync):
                        for gg in range(NGG):
                            sync.wait_ge(fvv, FVV0 + gg + 1)
                            sync.dma_start(out=yout[gg * 128:(gg + 1) * 128, :],
                                           in_=s_ost[gg % 2][:]).then_inc(ost, 16)

                    @blk.gpsimd
                    def _(g):
                        g.wait_ge(ost, NGG * 16)

    emit_ln(0, "plain", ext_stats=(p_s1_0, p_s2_0, es0),
            fuse_next=(p_s1_1x, p_s2_1x))
    emit_ln(1, "affine", ext_stats=(p_s1_1x, p_s2_1x, es1))
    if debug:
        with nc.Block() as blk:
            @blk.gpsimd
            def _(g):
                for gg in range(NGG):
                    g.wait_ge(dbgs, gg + 1)
                    g.dma_start(out=dbg_x1[gg], in_=s_x1[gg][:]).then_inc(yld, 16)
    if debug:
        with nc.Block() as blk:
            @blk.gpsimd
            def _(g):
                dsem = nc.semaphore("dsem5").__enter__()
                for gg in range(NGG):
                    g.dma_start(out=dbg_xn[gg], in_=s_xn[gg][:]).then_inc(dsem, 16)
                g.wait_ge(dsem, NGG * 16)

    pfs = ExitStack()
    p_f = [pfs.enter_context(nc.psum_tensor(f"p_f{i}", [128, 512], F32))
           for i in range(2)]
    with nc.Block() as blk:
        @blk.tensor
        def _(tensor):
            tensor.wait_ge(wql, NGG * 16)
            for og in range(NGG):
                for th in range(TSL // 512):
                    m = og * (TSL // 512) + th
                    pt = p_f[m % 2]
                    if m >= 2:
                        tensor.wait_ge(fvv, NGG + m - 1)
                    for k in range(NGG):
                        tensor.matmul(
                            pt[:], s_Wq[k][:, og * 128:(og + 1) * 128],
                            s_xn[k][:, th * 512:(th + 1) * 512],
                            start=(k == 0), stop=False)
                    tensor.matmul(pt[:], s_Wq[0][:, 0:128], s_z16[:],
                                  start=False, stop=False)
                    tensor.matmul(pt[:], s_Wq[0][:, 0:128], s_z16[:],
                                  start=False, stop=True).then_inc(fmm, 1)

        @blk.vector
        def _(vector):
            for og in range(NGG):
                for th in range(TSL // 512):
                    m = og * (TSL // 512) + th
                    vector.wait_ge(fmm, m + 1)
                    sl = slice(th * 512, (th + 1) * 512)
                    vector.scalar_tensor_tensor(
                        out=s_x1[og][:, sl], in0=s_x1[og][:, sl], scalar=1.0,
                        in1=p_f[m % 2][:], op0=ALU.mult, op1=ALU.add,
                    ).then_inc(fvv, 1)

    if debug:
        with nc.Block() as blk:
            @blk.gpsimd
            def _(g):
                dsem = nc.semaphore("dsem6").__enter__()
                for gg in range(NGG):
                    g.dma_start(out=dbg_r2[gg], in_=s_x1[gg][:]).then_inc(dsem, 16)
                g.wait_ge(dsem, NGG * 16)

    pfs.close()
    emit_ln(2, "final")

    if debug:
        with nc.Block() as blk2:
            @blk2.gpsimd
            def _(g):
                dsem7 = nc.semaphore("dsem7").__enter__()
                g.dma_start(out=dbg_v, in_=s_vec[:]).then_inc(dsem7, 16)
                g.dma_start(out=dbg_mi, in_=s_mi[:]).then_inc(dsem7, 16)
                g.wait_ge(dsem7, 32)

    ff.close()
    ex.close()
    return nc


_NC_CACHE = None


def kernel(L0_attn_x, L0_attn_z, L0_attn_W_bc, L0_attn_W_dt, L0_attn_b_dt,
           L0_attn_A_log, L0_attn_D_param, L0_ffn_W, L0_ffn_ln_w, L0_ffn_ln_b):
    global _NC_CACHE
    import ml_dtypes
    bf = ml_dtypes.bfloat16

    x2d = np.ascontiguousarray(np.asarray(L0_attn_x, dtype=np.float32)
                               .reshape(T, D))
    W_bc = np.asarray(L0_attn_W_bc, dtype=np.float32)
    W_dt = np.asarray(L0_attn_W_dt, dtype=np.float32)
    b_dt = np.asarray(L0_attn_b_dt, dtype=np.float32)
    A_log = np.asarray(L0_attn_A_log, dtype=np.float32)
    D_param = np.asarray(L0_attn_D_param, dtype=np.float32)
    W = np.ascontiguousarray(np.asarray(L0_ffn_W, dtype=np.float32))
    ln_w = np.asarray(L0_ffn_ln_w, dtype=np.float32)
    ln_b = np.asarray(L0_ffn_ln_b, dtype=np.float32)

    ident = np.eye(128, dtype=np.float32)
    in_maps = []
    for c in range(NCORES):
        dsl = slice(c * DSL, (c + 1) * DSL)
        tsl = slice(c * TSL, (c + 1) * TSL)
        in_maps.append(dict(
            xdsT=np.ascontiguousarray(x2d[:, dsl].T),
            xtsT=np.ascontiguousarray(x2d[tsl, :].T),
            wbcT=np.ascontiguousarray(W_bc[:, dsl].T),
            wdtT=np.ascontiguousarray(W_dt[dsl, :].T),
            alog=np.ascontiguousarray(A_log[dsl, :]),
            bdt=np.ascontiguousarray(b_dt[dsl].reshape(NDG, 128).T),
            dpar=np.ascontiguousarray(D_param[dsl].reshape(NDG, 128).T),
            lnw=np.ascontiguousarray(ln_w.reshape(NGG, 128).T),
            lnb=np.ascontiguousarray(ln_b.reshape(NGG, 128).T),
            wmat=W,
            ident16=ident.astype(bf),
            ones_k16=np.ones((128, 1), dtype=bf),
            ones_kf=np.ones((128, 1), dtype=np.float32),
            ones_mf=np.ones((1, 128), dtype=np.float32),
            zer16=np.zeros((128, 512), dtype=bf),
            zerf=np.zeros((128, 1), dtype=np.float32),
        ))

    if _NC_CACHE is None:
        _NC_CACHE = build_nc()
    res = run_bass_kernel_spmd(_NC_CACHE, in_maps, core_ids=list(range(NCORES)))

    out = np.empty((1, T, D), dtype=np.float32)
    for c in range(NCORES):
        out[0, c * TSL:(c + 1) * TSL, :] = res.results[c]["yout"].T
    return out


# version 14
# speedup vs baseline: 1.0617x; 1.0063x over previous
import numpy as np
from contextlib import ExitStack

import concourse.bass as bass
import concourse.mybir as mybir
from concourse.bass_utils import run_bass_kernel_spmd
from concourse.aot_env import allow_ici_all_to_all

F32 = mybir.dt.float32
BF16 = mybir.dt.bfloat16
AF = mybir.ActivationFunctionType
ALU = mybir.AluOpType
AXX = mybir.AxisListType.X

NCORES = 8
T = 8192
D = 2048
NST = 32
RK = 64
DSL = D // NCORES
TSL = T // NCORES
NDG = DSL // 128
NGG = D // 128
TC = 2048
NTC = T // TC
NIT = NTC * NST
MAGIC = 12582912.0
EPS = 1e-5


def build_nc(debug=False):
    nc = bass.Bass("TRN2", target_bir_lowering=False)

    xdsT = nc.dram_tensor("xdsT", [DSL, T], F32, kind="ExternalInput").ap()
    xtsT = nc.dram_tensor("xtsT", [D, TSL], F32, kind="ExternalInput").ap()
    wbcT = nc.dram_tensor("wbcT", [DSL, 128], F32, kind="ExternalInput").ap()
    wdtT = nc.dram_tensor("wdtT", [RK, DSL], F32, kind="ExternalInput").ap()
    alog = nc.dram_tensor("alog", [DSL, NST], F32, kind="ExternalInput").ap()
    bdt = nc.dram_tensor("bdt", [128, NDG], F32, kind="ExternalInput").ap()
    dpar = nc.dram_tensor("dpar", [128, NDG], F32, kind="ExternalInput").ap()
    lnw = nc.dram_tensor("lnw", [128, NGG], F32, kind="ExternalInput").ap()
    lnb = nc.dram_tensor("lnb", [128, NGG], F32, kind="ExternalInput").ap()
    wmat = nc.dram_tensor("wmat", [D, D], F32, kind="ExternalInput").ap()
    ident16 = nc.dram_tensor("ident16", [128, 128], BF16, kind="ExternalInput").ap()
    ones_k16 = nc.dram_tensor("ones_k16", [128, 1], BF16, kind="ExternalInput").ap()
    ones_kf = nc.dram_tensor("ones_kf", [128, 1], F32, kind="ExternalInput").ap()
    ones_mf = nc.dram_tensor("ones_mf", [1, 128], F32, kind="ExternalInput").ap()
    zer16 = nc.dram_tensor("zer16", [128, 512], BF16, kind="ExternalInput").ap()
    zerf = nc.dram_tensor("zerf", [128, 1], F32, kind="ExternalInput").ap()

    yout = nc.dram_tensor("yout", [D, TSL], F32, kind="ExternalOutput").ap()

    ar_i = nc.dram_tensor("ar_i", [128, T], F32)
    ar_o = nc.dram_tensor("ar_o", [128, T], F32)
    a2a_i = nc.dram_tensor("a2a_i", [NCORES, NDG, 128, TSL], BF16)
    a2a_o = nc.dram_tensor("a2a_o", [NCORES, NDG, 128, TSL], BF16)
    wq_d = nc.dram_tensor("wq_d", [NGG, 128, D], BF16)
    scr_d = nc.dram_tensor("scr_d", [128, 1], F32)
    bar_i = nc.dram_tensor("bar_i", [1, 512], F32)
    bar_o = nc.dram_tensor("bar_o", [1, 512], F32)
    if debug:
        dbg_dbc = nc.dram_tensor("dbg_dbc", [128, T], F32, kind="ExternalOutput").ap()
        dbg_dl = nc.dram_tensor("dbg_dl", [NDG, 128, T], BF16, kind="ExternalOutput").ap()
        dbg_w16 = nc.dram_tensor("dbg_w16", [NDG, 128, T], BF16, kind="ExternalOutput").ap()
        dbg_yi = nc.dram_tensor("dbg_yi", [NCORES, NDG, 128, TSL], BF16, kind="ExternalOutput").ap()
        dbg_yo = nc.dram_tensor("dbg_yo", [NCORES, NDG, 128, TSL], BF16, kind="ExternalOutput").ap()
        dbg_wq = nc.dram_tensor("dbg_wq", [NGG, 128, D], BF16, kind="ExternalOutput").ap()
        dbg_x1 = nc.dram_tensor("dbg_x1", [NGG, 128, TSL], F32, kind="ExternalOutput").ap()
        dbg_xn = nc.dram_tensor("dbg_xn", [NGG, 128, TSL], BF16, kind="ExternalOutput").ap()
        dbg_r2 = nc.dram_tensor("dbg_r2", [NGG, 128, TSL], F32, kind="ExternalOutput").ap()
        dbg_yr = nc.dram_tensor("dbg_yr", [NGG, 128, TSL], F32, kind="ExternalOutput").ap()
        dbg_v = nc.dram_tensor("dbg_v", [1, 3 * TSL], F32, kind="ExternalOutput").ap()
        dbg_mi = nc.dram_tensor("dbg_mi", [128, 2 * TSL], F32, kind="ExternalOutput").ap()

    ex = ExitStack()
    E = ex.enter_context

    s_An = [E(nc.sbuf_tensor(f"s_An{g}", [128, NST], F32)) for g in range(NDG)]
    s_bdt = E(nc.sbuf_tensor("s_bdt", [128, NDG], F32))
    s_dpar = E(nc.sbuf_tensor("s_dpar", [128, NDG], F32))
    s_lnw = E(nc.sbuf_tensor("s_lnw", [128, NGG], F32))
    s_lnb = E(nc.sbuf_tensor("s_lnb", [128, NGG], F32))
    s_id16 = E(nc.sbuf_tensor("s_id16", [128, 128], BF16))
    s_ok16 = E(nc.sbuf_tensor("s_ok16", [128, 1], BF16))
    s_okf = E(nc.sbuf_tensor("s_okf", [128, 1], F32))
    s_omf = E(nc.sbuf_tensor("s_omf", [1, 128], F32))
    s_sinv = E(nc.sbuf_tensor("s_sinv", [128, 1], F32))
    s_z16 = E(nc.sbuf_tensor("s_z16", [128, 512], BF16))
    s_zf = E(nc.sbuf_tensor("s_zf", [128, 1], F32))

    sems = {}
    for name in ["ld", "xc", "wld", "qn1", "qn2", "wqs", "dbc", "ste", "sse",
                 "sve", "svf", "ars", "ccs", "repB", "repC", "dAs", "vsc",
                 "tmm", "ysv", "ya", "yld", "wql", "fca", "fsq", "fte", "fvv",
                 "fmm", "ost", "dsm", "dbgs", "wldo", "lnr", "ldo2",
                 "rB0", "rB1", "rB2", "rB3", "rC0", "rC1", "rC2", "rC3"]:
        sems[name] = E(nc.semaphore(name))
    ld, xc, wld = sems["ld"], sems["xc"], sems["wld"]
    qn1, qn2, wqs = sems["qn1"], sems["qn2"], sems["wqs"]
    dbc, ste, sse = sems["dbc"], sems["ste"], sems["sse"]
    sve, svf, ars = sems["sve"], sems["svf"], sems["ars"]
    ccs, repB, repC = sems["ccs"], sems["repB"], sems["repC"]
    dAs, vsc, tmm = sems["dAs"], sems["vsc"], sems["tmm"]
    ysv, ya, yld = sems["ysv"], sems["ya"], sems["yld"]
    wql, fca, fsq = sems["wql"], sems["fca"], sems["fsq"]
    fte, fvv, fmm = sems["fte"], sems["fvv"], sems["fmm"]
    ost = sems["ost"]
    dsm = sems["dsm"]
    dbgs = sems["dbgs"]
    wldo = sems["wldo"]
    lnr = sems["lnr"]
    ldo2 = sems["ldo2"]
    repBs = [sems[f"rB{i}"] for i in range(4)]
    repCs = [sems[f"rC{i}"] for i in range(4)]

    ps = ExitStack()
    PS = ps.enter_context
    s_u16 = [PS(nc.sbuf_tensor(f"s_u16{g}", [128, T], BF16)) for g in range(NDG)]
    s_w16 = [PS(nc.sbuf_tensor(f"s_w16{g}", [128, T], BF16)) for g in range(NDG)]
    s_dl16 = [PS(nc.sbuf_tensor(f"s_dl16{g}", [128, T], BF16)) for g in range(NDG)]
    s_B16 = PS(nc.sbuf_tensor("s_B16", [128, TC], BF16))
    s_C16 = PS(nc.sbuf_tensor("s_C16", [128, TC], BF16))
    s_wbcT = [PS(nc.sbuf_tensor(f"s_wbcT{g}", [128, 128], BF16)) for g in range(NDG)]
    s_wdtT = PS(nc.sbuf_tensor("s_wdtT", [RK, DSL], F32))
    s_st = PS(nc.sbuf_tensor("s_st", [128, NDG * NST], F32))

    pre = ExitStack()
    PR = pre.enter_context
    s_dbcF = PR(nc.sbuf_tensor("s_dbcF", [128, T], F32))
    s_sp1 = PR(nc.sbuf_tensor("s_sp1", [128, TC], F32))
    s_sp2 = PR(nc.sbuf_tensor("s_sp2", [128, TC], F32))
    s_xld = s_sp1
    s_wt = [PR(nc.sbuf_tensor(f"s_wt{i}", [128, D], F32)) for i in range(2)]
    s_wt2 = [PR(nc.sbuf_tensor(f"s_wt2{i}", [128, D], F32)) for i in range(2)]
    s_wq = [PR(nc.sbuf_tensor(f"s_wq{i}", [128, D], BF16)) for i in range(2)]
    s_abs = PR(nc.sbuf_tensor("s_abs", [128, NGG + 3], F32))
    s_sc1 = PR(nc.sbuf_tensor("s_sc1", [1, 4], F32))
    s_row = PR(nc.sbuf_tensor("s_row", [1, 128], F32))
    p_a = PR(nc.psum_tensor("p_a", [128, 512], F32))
    p_b = PR(nc.psum_tensor("p_b", [128, 512], F32))
    p_s = PR(nc.psum_tensor("p_s", [128, 2], F32))

    N_CONST_DMA = (11 + NDG + 1) * 16
    STE0 = NGG + 4
    NCH = NDG * NTC

    _svf_n = [0]
    _svf_xb = {}
    _svf_abs = {}

    def _svf_next(kind, key):
        _svf_n[0] += 1
        if kind == "xb":
            _svf_xb[key] = _svf_n[0]
        elif kind == "abs":
            _svf_abs[key] = _svf_n[0]
        return _svf_n[0]

    _sp_order = []
    _sp_order += [("xb", (0, 0)), ("xb", (0, 1)), ("abs", (0, 0)),
                  ("xb", (0, 2)), ("xb", (0, 3)), ("abs", (0, 1))]
    for c in range(NCH - 1):
        _sp_order += [("d16", (c, 0)),
                      ("xb", (c + 1, 0)), ("xb", (c + 1, 1)), ("abs", (c + 1, 0)),
                      ("d16", (c, 1)),
                      ("xb", (c + 1, 2)), ("xb", (c + 1, 3)), ("abs", (c + 1, 1))]
    _sp_order += [("d16", (NCH - 1, 0)), ("d16", (NCH - 1, 1))]
    for kind, key in _sp_order:
        if kind != "d16":
            _svf_next(kind, key)
    _svf_n[0] = 0

    with nc.Block() as blk:
        @blk.sync
        def _(sync):
            loads = [
                (s_wdtT[:], wdtT),
                (s_bdt[:], bdt), (s_dpar[:], dpar),
                (s_lnw[:], lnw), (s_lnb[:], lnb),
                (s_id16[:], ident16), (s_ok16[:], ones_k16),
                (s_okf[:], ones_kf), (s_omf[:], ones_mf),
                (s_z16[:], zer16), (s_zf[:], zerf),
            ]
            for g in range(NDG):
                loads.append((s_An[g][:], alog[g * 128:(g + 1) * 128, :]))
            loads.append((
                s_sp2[:, 0:NDG * 128].rearrange("p (g q) -> p g q", q=128),
                wbcT.rearrange("(g p) q -> p g q", p=128)))
            for o, i in loads:
                sync.dma_start(out=o, in_=i).then_inc(ld, 16)
            for g in range(NDG):
                for q in range(NTC):
                    k = g * NTC + q
                    if k > 0:
                        sync.wait_ge(xc, k)
                    sync.dma_start(
                        out=s_xld[:],
                        in_=xdsT[g * 128:(g + 1) * 128, q * TC:(q + 1) * TC],
                    ).then_inc(ld, 16)
            for k in range(NGG):
                if k >= 2:
                    sync.wait_ge(qn1, k - 1)
                sync.dma_start(out=s_wt[k % 2][:],
                               in_=wmat[k * 128:(k + 1) * 128, :]).then_inc(wld, 16)
            for k in range(0, NGG, 2):
                j = k // 2
                if k == 0:
                    sync.wait_ge(qn1, NGG - 1)
                elif k >= 4:
                    sync.wait_ge(qn2, k - 3)
                buf = s_wt[0] if j % 2 == 0 else s_wt2[0]
                sync.dma_start(out=buf[:],
                               in_=wmat[k * 128:(k + 1) * 128, :]).then_inc(wld, 16)
            sync.wait_ge(qn2, NGG)
            for k in range(2):
                sync.dma_start(out=s_wt[k][:],
                               in_=wmat[k * 128:(k + 1) * 128, :]).then_inc(wld, 16)

        @blk.gpsimd
        def _(g):
            g.wait_ge(dbc, NGG)
            g.dma_start(out=ar_i.ap(), in_=s_dbcF[:]).then_inc(ars, 16)
            g.wait_ge(ars, 16)
            g.collective_compute(
                "AllReduce", ALU.add,
                replica_groups=[list(range(NCORES))],
                ins=[ar_i.ap().opt()], outs=[ar_o.ap().opt()],
            ).then_inc(ccs, 1)
            g.wait_ge(ccs, 1)
            g.dma_start(out=s_dbcF[:], in_=ar_o.ap()).then_inc(ars, 16)
            for k in range(NGG):
                g.wait_ge(qn2, k + 1)
                g.dma_start(out=wq_d.ap()[k], in_=s_wq[k % 2][:]).then_inc(wqs, 16)
            for k in range(2):
                g.wait_ge(qn2, NGG + k + 1)
                g.dma_start(out=wq_d.ap()[k], in_=s_wq[k][:]).then_inc(wqs, 16)

        @blk.vector
        def _(vector):
            vector.wait_ge(ld, N_CONST_DMA)
            for g in range(NDG):
                for q in range(NTC):
                    k = g * NTC + q
                    vector.wait_ge(ld, N_CONST_DMA + (k + 1) * 16)
                    vector.tensor_copy(
                        out=s_u16[g][:, q * TC:(q + 1) * TC], in_=s_xld[:]
                    ).then_inc(xc, 1)
            for g in range(NDG):
                vector.tensor_copy(out=s_wbcT[g][:],
                                   in_=s_sp2[:, g * 128:(g + 1) * 128])
            vector.wait_ge(sse, 1)
            for g in range(NDG):
                ins = vector.tensor_scalar_mul(out=s_An[g][:], in0=s_An[g][:],
                                               scalar1=-1.0)
            ins.then_inc(sve, 1)

            for tt in range(NGG):
                vector.wait_ge(ste, tt + 1)
                pt = p_a if tt % 2 == 0 else p_b
                vector.tensor_copy(out=s_dbcF[:, tt * 512:(tt + 1) * 512],
                                   in_=pt[:]).then_inc(dbc, 1)

            for k in range(NGG):
                vector.wait_ge(wld, (k + 1) * 16)
                vector.tensor_reduce(
                    out=s_abs[:, k:k + 1], in_=s_wt[k % 2][:], axis=AXX,
                    op=ALU.add, apply_absolute_value=True,
                ).then_inc(qn1, 1)
            vector.tensor_reduce(
                out=s_abs[:, NGG + 1:NGG + 2], in_=s_wt[(NGG - 1) % 2][:],
                axis=AXX, op=ALU.add, apply_absolute_value=True)
            vector.reduce_sum(out=s_abs[:, NGG:NGG + 1], in_=s_abs[:, 0:NGG],
                              axis=AXX)
            vector.reduce_sum(out=s_abs[:, NGG:NGG + 1], in_=s_abs[:, 0:NGG],
                              axis=AXX).then_inc(sve, 1)
            vector.wait_ge(ste, NGG + 2)
            vector.tensor_scalar_mul(out=s_sc1[:, 0:1], in0=p_s[0:1, 0:1],
                                     scalar1=1.0 / (D * D))
            vector.tensor_scalar_mul(out=s_sc1[:, 0:1], in0=p_s[0:1, 0:1],
                                     scalar1=1.0 / (D * D))
            vector.tensor_scalar_mul(out=s_sc1[:, 0:1], in0=p_s[0:1, 0:1],
                                     scalar1=1.0 / (D * D)).then_inc(sve, 1)
            vector.wait_ge(ste, NGG + 4)
            vector.tensor_copy(out=s_sinv[:], in_=p_s[:, 0:1])
            vector.tensor_copy(out=s_sinv[:], in_=p_s[:, 0:1])
            vector.tensor_copy(out=s_sinv[:], in_=p_s[:, 0:1])
            vector.wait_ge(ars, 32)
            for tcq in range(NTC):
                vector.tensor_copy(out=s_B16[tcq * NST:(tcq + 1) * NST, :],
                                   in_=s_dbcF[64:96, tcq * TC:(tcq + 1) * TC])
                vector.tensor_copy(out=s_C16[tcq * NST:(tcq + 1) * NST, :],
                                   in_=s_dbcF[96:128, tcq * TC:(tcq + 1) * TC])
            for kind, key in _sp_order:
                if kind == "xb":
                    c, j = key
                    g = c // NTC
                    m = 4 * c + j
                    vector.wait_ge(ste, STE0 + m + 1)
                    pt = p_a if m % 2 == 0 else p_b
                    vector.tensor_scalar(
                        out=s_sp1[:, j * 512:(j + 1) * 512], in0=pt[:],
                        scalar1=s_bdt[:, g:g + 1], scalar2=None,
                        op0=ALU.add).then_inc(svf, 1)
                elif kind == "abs":
                    c, h = key
                    hs = slice(h * 1024, (h + 1) * 1024)
                    vector.scalar_tensor_tensor(
                        out=s_sp2[:, hs], in0=s_sp1[:, hs], scalar=-1.0,
                        in1=s_sp1[:, hs],
                        op0=ALU.mult, op1=ALU.min).then_inc(svf, 1)
                else:
                    c, h = key
                    g, q = divmod(c, NTC)
                    hs = slice(h * 1024, (h + 1) * 1024)
                    vector.wait_ge(sse, 2 + 2 * c + h + 1)
                    vector.scalar_tensor_tensor(
                        out=s_dl16[g][:, q * TC + h * 1024:q * TC + (h + 1) * 1024],
                        in0=s_sp1[:, hs], scalar=0.0, in1=s_sp2[:, hs],
                        op0=ALU.max, op1=ALU.add)
            for g in range(NDG):
                vector.tensor_mul(out=s_w16[g][:], in0=s_dl16[g][:],
                                  in1=s_u16[g][:])
            for k in range(NGG):
                j = k // 2
                if k % 2 == 0:
                    vector.wait_ge(wld, (NGG + j + 1) * 16)
                    wbuf = s_wt[0] if j % 2 == 0 else s_wt2[0]
                else:
                    vector.wait_ge(wldo, (j + 1) * 16)
                    wbuf = s_wt[1] if j % 2 == 0 else s_wt2[1]
                if k >= 2:
                    vector.wait_ge(wqs, (k - 1) * 16)
                vector.tensor_scalar(out=s_sp1[:, 0:D], in0=wbuf[:],
                                     scalar1=s_sinv[:], scalar2=MAGIC,
                                     op0=ALU.mult, op1=ALU.add)
                vector.tensor_scalar(out=s_sp1[:, 0:D], in0=s_sp1[:, 0:D],
                                     scalar1=MAGIC, scalar2=1.0,
                                     op0=ALU.subtract, op1=ALU.min)
                vector.tensor_scalar_max(out=s_wq[k % 2][:], in0=s_sp1[:, 0:D],
                                         scalar1=-1.0).then_inc(qn2, 1)
            for k in range(2):
                vector.wait_ge(wld, (NGG + NGG // 2 + k + 1) * 16)
                vector.wait_ge(wqs, NGG * 16)
                vector.tensor_scalar(out=s_sp1[:, 0:D], in0=s_wt[k][:],
                                     scalar1=s_sinv[:], scalar2=MAGIC,
                                     op0=ALU.mult, op1=ALU.add)
                vector.tensor_scalar(out=s_sp1[:, 0:D], in0=s_sp1[:, 0:D],
                                     scalar1=MAGIC, scalar2=1.0,
                                     op0=ALU.subtract, op1=ALU.min)
                vector.tensor_scalar_max(out=s_wq[k][:], in0=s_sp1[:, 0:D],
                                         scalar1=-1.0).then_inc(qn2, 1)

        @blk.scalar
        def _(scalar):
            scalar.wait_ge(ld, N_CONST_DMA)
            for g in range(NDG):
                ins = scalar.activation(out=s_An[g][:], in_=s_An[g][:],
                                        func=AF.Exp)
            ins.then_inc(sse, 1)
            scalar.wait_ge(sve, 3)
            scalar.activation(out=s_sc1[:, 1:2], in_=s_sc1[:, 0:1], func=AF.Ln)
            scalar.activation(out=s_sc1[:, 1:2], in_=s_sc1[:, 0:1], func=AF.Ln)
            scalar.activation(out=s_sc1[:, 2:3], in_=s_sc1[:, 1:2],
                              func=AF.Exp, scale=-1.0)
            scalar.activation(out=s_sc1[:, 2:3], in_=s_sc1[:, 1:2],
                              func=AF.Exp, scale=-1.0).then_inc(sse, 1)
            for c in range(NCH):
                for h in range(2):
                    hs = slice(h * 1024, (h + 1) * 1024)
                    scalar.wait_ge(svf, _svf_abs[(c, h)])
                    scalar.activation(out=s_sp2[:, hs], in_=s_sp2[:, hs],
                                      func=AF.Exp)
                    scalar.activation(out=s_sp2[:, hs], in_=s_sp2[:, hs],
                                      func=AF.Ln, bias=1.0).then_inc(sse, 1)
            for k in range(1, NGG, 2):
                j = k // 2
                if k == 1:
                    scalar.wait_ge(qn1, NGG)
                elif k >= 5:
                    scalar.wait_ge(qn2, k - 3)
                buf = s_wt[1] if j % 2 == 0 else s_wt2[1]
                scalar.dma_start(out=buf[:],
                                 in_=wmat[k * 128:(k + 1) * 128, :]
                                 ).then_inc(wldo, 16)

        @blk.tensor
        def _(tensor):
            tensor.wait_ge(sve, 1)
            for tt in range(NGG):
                pt = p_a if tt % 2 == 0 else p_b
                if tt >= 2:
                    tensor.wait_ge(dbc, tt - 1)
                for g in range(NDG):
                    tensor.matmul(pt[:], s_wbcT[g][:],
                                  s_u16[g][:, tt * 512:(tt + 1) * 512],
                                  start=(g == 0), stop=False)
                tensor.matmul(pt[:], s_wbcT[0][:], s_z16[:],
                              start=False, stop=True).then_inc(ste, 1)
            tensor.wait_ge(sve, 2)
            tensor.matmul(p_s[0:1, 0:1], s_okf[:], s_abs[:, NGG:NGG + 1],
                          start=True, stop=False)
            tensor.matmul(p_s[0:1, 0:1], s_okf[:], s_zf[:, 0:1],
                          start=False, stop=False)
            tensor.matmul(p_s[0:1, 0:1], s_okf[:], s_zf[:, 0:1],
                          start=False, stop=True).then_inc(ste, 2)
            tensor.wait_ge(sse, 2)
            tensor.matmul(p_s[:, 0:1], s_omf[:], s_sc1[0:1, 2:3],
                          start=True, stop=False)
            tensor.matmul(p_s[:, 0:1], s_omf[:], s_zf[0:1, 0:1],
                          start=False, stop=False)
            tensor.matmul(p_s[:, 0:1], s_omf[:], s_zf[0:1, 0:1],
                          start=False, stop=True).then_inc(ste, 2)
            tensor.wait_ge(ars, 32)
            for m in range(NDG * NGG):
                g, tt4 = divmod(m, NGG)
                pt = p_a if m % 2 == 0 else p_b
                if m >= 2:
                    mm = m - 2
                    tensor.wait_ge(svf, _svf_xb[(mm // 4, mm % 4)])
                tensor.matmul(pt[:], s_wdtT[:, g * 128:(g + 1) * 128],
                              s_dbcF[0:RK, tt4 * 512:(tt4 + 1) * 512],
                              start=True, stop=True).then_inc(ste, 1)


    if debug:
        with nc.Block() as blk:
            @blk.gpsimd
            def _(g):
                dsem = nc.semaphore("dsem1").__enter__()
                g.dma_start(out=dbg_dbc, in_=s_dbcF[:]).then_inc(dsem, 16)
                for gg in range(NDG):
                    g.dma_start(out=dbg_dl[gg], in_=s_dl16[gg][:]).then_inc(dsem, 16)
                    g.dma_start(out=dbg_w16[gg], in_=s_w16[gg][:]).then_inc(dsem, 16)
                g.wait_ge(dsem, (1 + 2 * NDG) * 16)

    pre.close()

    sc = ExitStack()
    SC = sc.enter_context
    s_Brep = [SC(nc.sbuf_tensor(f"s_Brep{i}", [128, TC], BF16)) for i in range(4)]
    s_Crep = [SC(nc.sbuf_tensor(f"s_Crep{i}", [128, TC], BF16)) for i in range(4)]
    s_dA = [SC(nc.sbuf_tensor(f"s_dA{g}", [128, TC], F32)) for g in range(NDG)]
    s_wb = SC(nc.sbuf_tensor("s_wb", [128, TC], BF16))
    s_h = SC(nc.sbuf_tensor("s_h", [128, TC], BF16))
    s_hc = [SC(nc.sbuf_tensor(f"s_hc{i}", [128, TC], BF16)) for i in range(2)]
    s_ys = [SC(nc.sbuf_tensor(f"s_ys{g}", [128, TC], BF16)) for g in range(NDG)]
    p_y = [SC(nc.psum_tensor(f"p_y{g}", [128, TC], F32)) for g in range(NDG)]

    def chains_interleaved(eng, bufs, rows, sem, gate, it_range=None, state=None,
                           group=2):
        depth = len(bufs)
        steps = [(0, 1)]
        k = 1
        while k < 128:
            c = min(k, 128 - k)
            steps.append((k, c))
            k += c
        nst = len(steps)
        if state is None:
            state = [0]
        if it_range is None:
            it_range = range(0, NIT)
        lo, hi = it_range.start, it_range.stop
        for itg in range(lo, hi, group):
            grp = [itg + j for j in range(group) if itg + j < hi]
            for s in range(nst):
                for it in grp:
                    slot = it % depth
                    base = (it // depth) * nst * 16
                    if s == 0:
                        gate(eng, it)
                        eng.dma_start(out=bufs[slot][0:1, :],
                                      in_=rows[it]).then_inc(sem[slot], 16)
                    else:
                        k0, c = steps[s]
                        eng.wait_ge(sem[slot], base + s * 16)
                        eng.dma_start(out=bufs[slot][k0:k0 + c, :],
                                      in_=bufs[slot][0:c, :]
                                      ).then_inc(sem[slot], 16)

    with nc.Block() as blk:
        @blk.sync
        def _(sync):
            rowsB = []
            for it in range(NIT):
                tc, n = divmod(it, NST)
                rowsB.append(s_B16[tc * NST + n:tc * NST + n + 1, :])

            def gateB(eng, it):
                if it >= 4:
                    eng.wait_ge(vsc, (it - 3) * NDG)
            chains_interleaved(sync, s_Brep, rowsB, repBs, gateB)

        @blk.gpsimd
        def _(g):
            rowsC = []
            for it in range(NIT):
                tc, n = divmod(it, NST)
                rowsC.append(s_C16[tc * NST + n:tc * NST + n + 1, :])

            def gateC(eng, it):
                if it >= 4:
                    eng.wait_ge(vsc, (it - 3) * NDG)
            for tc in range(NTC):
                chains_interleaved(g, s_Crep, rowsC, repCs, gateC,
                                   it_range=range(tc * NST, (tc + 1) * NST))
                for dg in range(NDG):
                    g.wait_ge(ysv, tc * NDG + dg + 1)
                    for h in range(TC // TSL):
                        dest = tc * (TC // TSL) + h
                        g.dma_start(
                            out=a2a_i.ap()[dest, dg],
                            in_=s_ys[dg][:, h * TSL:(h + 1) * TSL],
                        ).then_inc(ya, 16)

        @blk.scalar
        def _(scalar):
            for it in range(NIT):
                tc, n = divmod(it, NST)
                for g in range(NDG):
                    idx = it * NDG + g
                    if idx >= NDG:
                        scalar.wait_ge(vsc, idx - NDG + 1)
                    scalar.activation(
                        out=s_dA[g][:], in_=s_dl16[g][:, tc * TC:(tc + 1) * TC],
                        func=AF.Exp, scale=s_An[g][:, n:n + 1],
                    ).then_inc(dAs, 1)

        @blk.vector
        def _(vector):
            for tc in range(NTC):
                for n in range(NST):
                    it = tc * NST + n
                    vector.wait_ge(repBs[it % 4], 128 * (it // 4 + 1))
                    vector.wait_ge(repCs[it % 4], 128 * (it // 4 + 1))
                    for g in range(NDG):
                        idx = it * NDG + g
                        col = g * NST + n
                        vector.wait_ge(dAs, idx + 1)
                        vector.tensor_mul(out=s_wb[:],
                                          in0=s_w16[g][:, tc * TC:(tc + 1) * TC],
                                          in1=s_Brep[it % 4][:])
                        init = 0.0 if tc == 0 else s_st[:, col:col + 1]
                        vector.tensor_tensor_scan(
                            out=s_h[:], data0=s_dA[g][:], data1=s_wb[:],
                            initial=init, op0=ALU.mult, op1=ALU.add)
                        if idx >= 2:
                            vector.wait_ge(tmm, idx - 1)
                        vector.tensor_mul(out=s_hc[idx % 2][:], in0=s_h[:],
                                          in1=s_Crep[it % 4][:]).then_inc(vsc, 1)
                        vector.tensor_copy(out=s_st[:, col:col + 1],
                                           in_=s_h[:, TC - 1:TC])
                for g in range(NDG):
                    vector.wait_ge(tmm, (tc * NST + NST - 1) * NDG + g + 1)
                    if tc >= 1:
                        vector.wait_ge(ya, (tc - 1) * NDG * 32 + (g + 1) * 32)
                    vector.scalar_tensor_tensor(
                        out=s_ys[g][:], in0=s_u16[g][:, tc * TC:(tc + 1) * TC],
                        scalar=s_dpar[:, g:g + 1], in1=p_y[g][:],
                        op0=ALU.mult, op1=ALU.add).then_inc(ysv, 1)

        @blk.tensor
        def _(tensor):
            for it in range(NIT):
                tc, n = divmod(it, NST)
                for g in range(NDG):
                    idx = it * NDG + g
                    tensor.wait_ge(vsc, idx + 1)
                    if n == 0 and tc >= 1:
                        tensor.wait_ge(ysv, (tc - 1) * NDG + g + 1)
                    for j in range(TC // 512):
                        tensor.matmul(
                            p_y[g][:, j * 512:(j + 1) * 512], s_id16[:],
                            s_hc[idx % 2][:, j * 512:(j + 1) * 512],
                            start=(n == 0), stop=False)
                    if n == NST - 1:
                        for j in range(TC // 512):
                            ins = tensor.matmul(
                                p_y[g][:, j * 512:(j + 1) * 512], s_id16[:],
                                s_z16[:], start=False, stop=True)
                        ins.then_inc(tmm, 1)
                    else:
                        ins = tensor.matmul(
                            p_y[g][:, 0:512], s_id16[:], s_z16[:],
                            start=False, stop=False)
                        ins.then_inc(tmm, 1)

    if debug:
        with nc.Block() as blk:
            @blk.gpsimd
            def _(g):
                dsem = nc.semaphore("dsem2").__enter__()
                g.dma_start(out=dbg_yi, in_=a2a_i.ap()).then_inc(dsem, 16)
                g.dma_start(out=dbg_wq, in_=wq_d.ap()).then_inc(dsem, 16)
                g.wait_ge(dsem, 32)

    sc.close()
    ps.close()

    ff = ExitStack()
    FF = ff.enter_context
    s_x1 = [FF(nc.sbuf_tensor(f"s_x1{g}", [128, TSL], F32)) for g in range(NGG)]
    s_Wq = [FF(nc.sbuf_tensor(f"s_Wq{g}", [128, D], BF16)) for g in range(NGG)]
    s_xn = [FF(nc.sbuf_tensor(f"s_xn{g}", [128, TSL], BF16)) for g in range(NGG)]
    s_yr = [FF(nc.sbuf_tensor(f"s_yr{i}", [128, TSL], BF16)) for i in range(2)]
    s_r16 = [FF(nc.sbuf_tensor(f"s_r16{i}", [128, TSL], BF16)) for i in range(3)]
    s_q16 = [FF(nc.sbuf_tensor(f"s_q16{i}", [128, TSL], BF16)) for i in range(3)]
    s_q16d = FF(nc.sbuf_tensor("s_q16d", [128, 512], BF16))
    s_mi = FF(nc.sbuf_tensor("s_mi", [128, 2 * TSL], F32))
    s_vec = FF(nc.sbuf_tensor("s_vec", [1, 3 * TSL], F32))
    s_ost = [FF(nc.sbuf_tensor(f"s_ost{i}", [128, TSL], F32)) for i in range(2)]

    LD0 = N_CONST_DMA + NDG * NTC * 16

    es1 = ExitStack()
    p_s1_1x = es1.enter_context(nc.psum_tensor("p_s1_1x", [1, TSL], F32))
    p_s2_1x = es1.enter_context(nc.psum_tensor("p_s2_1x", [1, TSL], F32))
    es0 = ExitStack()
    p_s1_0 = es0.enter_context(nc.psum_tensor("p_s1_0", [1, TSL], F32))
    p_s2_0 = es0.enter_context(nc.psum_tensor("p_s2_0", [1, TSL], F32))
    with nc.Block() as blk:
        @blk.gpsimd
        def _(g):
            g.wait_ge(ya, NTC * NDG * 2 * 16)
            with allow_ici_all_to_all():
                g.collective_compute(
                    "AllToAll", ALU.bypass,
                    replica_groups=[list(range(NCORES))],
                    ins=[a2a_i.ap().opt()], outs=[a2a_o.ap().opt()],
                ).then_inc(ccs, 1)
            g.wait_ge(ccs, 2)
            for gg in range(NGG):
                src, dg = divmod(gg, NDG)
                if gg >= 2:
                    g.wait_ge(fvv, gg - 1)
                g.dma_start(out=s_yr[gg % 2][:],
                            in_=a2a_o.ap()[src, dg]).then_inc(yld, 16)
            g.dma_start(out=s_ost[0][:], in_=a2a_o.ap()[0, 0]).then_inc(yld, 16)

        @blk.sync
        def _(sync):
            for gg in range(0, NGG, 2):
                sync.dma_start(out=s_x1[gg][:],
                               in_=xtsT[gg * 128:(gg + 1) * 128, :]).then_inc(ld, 16)
            sync.dma_start(out=s_ost[1][:],
                           in_=xtsT[0:128, :]).then_inc(ld, 16)
            sync.wait_ge(wqs, (NGG + 2) * 16)
            sync.wait_ge(fvv, NGG)
            for gg in range(NGG):
                sync.dma_start(out=s_Wq[gg][:], in_=wq_d.ap()[gg]).then_inc(wql, 16)

        @blk.vector
        def _(vector):
            for gg in range(NGG):
                j = gg // 2
                if gg % 2 == 0:
                    vector.wait_ge(ld, LD0 + min(j + 2, NGG // 2 + 1) * 16)
                else:
                    vector.wait_ge(ldo2, min(j + 2, NGG // 2 + 1) * 16)
                vector.wait_ge(yld, min(gg + 2, NGG + 1) * 16)
                vector.tensor_add(out=s_x1[gg][:], in0=s_x1[gg][:],
                                  in1=s_yr[gg % 2][:]).then_inc(fvv, 1)
                if gg >= 3:
                    vector.wait_ge(fte, gg - 2)
                vector.tensor_copy(out=s_r16[gg % 3][:],
                                   in_=s_x1[gg][:]).then_inc(fca, 1)
            vector.tensor_copy(out=s_q16d[:],
                               in_=s_x1[0][:, 0:512]).then_inc(fca, 1)

        @blk.scalar
        def _(scalar):
            for gg in range(1, NGG, 2):
                scalar.dma_start(out=s_x1[gg][:],
                                 in_=xtsT[gg * 128:(gg + 1) * 128, :]
                                 ).then_inc(ldo2, 16)
            scalar.dma_start(out=s_mi[:, 0:TSL],
                             in_=xtsT[0:128, :]).then_inc(ldo2, 16)
            for gg in range(NGG):
                scalar.wait_ge(fvv, gg + 1)
                if gg >= 3:
                    scalar.wait_ge(fte, gg - 2)
                scalar.activation(out=s_q16[gg % 3][:], in_=s_x1[gg][:],
                                  func=AF.Square).then_inc(fsq, 1)
            scalar.activation(out=s_q16d[:], in_=s_x1[0][:, 0:512],
                              func=AF.Square).then_inc(fsq, 1)

        @blk.tensor
        def _(tensor):
            for gg in range(NGG):
                tensor.wait_ge(fsq, min(gg + 2, NGG + 1))
                tensor.wait_ge(fca, min(gg + 2, NGG + 1))
                for h in range(TSL // 512):
                    sl = slice(h * 512, (h + 1) * 512)
                    tensor.matmul(p_s1_0[0:1, sl], s_ok16[:],
                                  s_r16[gg % 3][:, sl],
                                  start=(gg == 0), stop=False)
                    ins = tensor.matmul(p_s2_0[0:1, sl], s_ok16[:],
                                        s_q16[gg % 3][:, sl],
                                        start=(gg == 0), stop=False)
                    if h == TSL // 512 - 1:
                        ins.then_inc(fte, 1)
            for rnd in range(2):
                for h2 in range(TSL // 512):
                    sl2 = slice(h2 * 512, (h2 + 1) * 512)
                    tensor.matmul(p_s1_0[0:1, sl2], s_ok16[:], s_z16[:],
                                  start=False, stop=(rnd == 1))
                    tensor.matmul(p_s2_0[0:1, sl2], s_ok16[:], s_z16[:],
                                  start=False, stop=(rnd == 1))
        if debug:
            @blk.gpsimd
            def _(g):
                g.wait_ge(fvv, NGG)
                for gg in range(NGG):
                    g.dma_start(out=dbg_yr[gg], in_=s_x1[gg][:]).then_inc(yld, 16)
    if debug:
        with nc.Block() as blk:
            @blk.gpsimd
            def _(g):
                dsem = nc.semaphore("dsem3").__enter__()
                g.dma_start(out=dbg_yo, in_=a2a_o.ap()).then_inc(dsem, 16)
                g.wait_ge(dsem, 16)

    def emit_ln(idx, mode, ext_stats=None, fuse_next=None):
        off = idx * NGG
        off17 = idx * (NGG + 1)
        offn = (idx + 1) * NGG
        off17n = (idx + 1) * (NGG + 1)
        with ExitStack() as _les:
            if ext_stats is not None:
                p_s1, p_s2, _sst = ext_stats
            else:
                _sst = ExitStack()
                p_s1 = _sst.enter_context(
                    nc.psum_tensor(f"p_s1_{idx}", [1, TSL], F32))
                p_s2 = _sst.enter_context(
                    nc.psum_tensor(f"p_s2_{idx}", [1, TSL], F32))
            if ext_stats is None:
              with nc.Block() as blk:
                @blk.scalar
                def _(scalar):
                    for gg in range(NGG):
                        if gg >= 3:
                            scalar.wait_ge(fte, off + gg - 2)
                        scalar.activation(out=s_q16[gg % 3][:], in_=s_x1[gg][:],
                                          func=AF.Square).then_inc(fsq, 1)
                    scalar.activation(out=s_q16d[:], in_=s_x1[0][:, 0:512],
                                      func=AF.Square).then_inc(fsq, 1)

                @blk.vector
                def _(vector):
                    for gg in range(NGG):
                        if gg >= 3:
                            vector.wait_ge(fte, off + gg - 2)
                        vector.tensor_copy(out=s_r16[gg % 3][:],
                                           in_=s_x1[gg][:]).then_inc(fca, 1)
                    vector.tensor_copy(out=s_q16d[:],
                                       in_=s_x1[0][:, 0:512]).then_inc(fca, 1)

                @blk.tensor
                def _(tensor):
                    for gg in range(NGG):
                        tensor.wait_ge(fsq, off17 + min(gg + 2, NGG + 1))
                        tensor.wait_ge(fca, off17 + min(gg + 2, NGG + 1))
                        for h in range(TSL // 512):
                            sl = slice(h * 512, (h + 1) * 512)
                            tensor.matmul(p_s1[0:1, sl], s_ok16[:],
                                          s_r16[gg % 3][:, sl],
                                          start=(gg == 0), stop=False)
                            ins = tensor.matmul(p_s2[0:1, sl], s_ok16[:],
                                                s_q16[gg % 3][:, sl],
                                                start=(gg == 0), stop=False)
                            if h == TSL // 512 - 1:
                                ins.then_inc(fte, 1)
                    for rnd in range(2):
                        for h2 in range(TSL // 512):
                            sl2 = slice(h2 * 512, (h2 + 1) * 512)
                            tensor.matmul(p_s1[0:1, sl2], s_ok16[:], s_z16[:],
                                          start=False, stop=(rnd == 1))
                            tensor.matmul(p_s2[0:1, sl2], s_ok16[:], s_z16[:],
                                          start=False, stop=(rnd == 1))

            with nc.Block() as blk:
                @blk.vector
                def _(vector):
                    vector.tensor_scalar_mul(out=s_vec[:, 0:TSL],
                                             in0=p_s1[0:1, :], scalar1=1.0 / D)
                    vector.tensor_scalar_mul(out=s_vec[:, TSL:2 * TSL],
                                             in0=p_s2[0:1, :],
                                             scalar1=1.0 / D)
                    vector.tensor_mul(out=s_vec[:, 2 * TSL:3 * TSL],
                                      in0=s_vec[:, 0:TSL], in1=s_vec[:, 0:TSL])
                    vector.tensor_sub(out=s_vec[:, TSL:2 * TSL],
                                      in0=s_vec[:, TSL:2 * TSL],
                                      in1=s_vec[:, 2 * TSL:3 * TSL])
                    vector.tensor_scalar_add(out=s_vec[:, 2 * TSL:3 * TSL],
                                             in0=s_vec[:, TSL:2 * TSL],
                                             scalar1=EPS)
                    vector.tensor_scalar_add(out=s_vec[:, 2 * TSL:3 * TSL],
                                             in0=s_vec[:, TSL:2 * TSL],
                                             scalar1=EPS)
                    vector.tensor_scalar_mul(out=s_vec[:, 0:TSL],
                                             in0=p_s1[0:1, :], scalar1=1.0 / D)

            _sst.close()
            p_rp = _les.enter_context(
                nc.psum_tensor(f"p_rp_{idx}", [128, TSL], F32))
            p_rp2 = _les.enter_context(
                nc.psum_tensor(f"p_rp2_{idx}", [128, TSL], F32))

            with nc.Block() as blk:
                @blk.scalar
                def _(scalar):
                    scalar.activation(out=s_vec[:, TSL:2 * TSL],
                                      in_=s_vec[:, 2 * TSL:3 * TSL],
                                      func=AF.Ln)
                    scalar.activation(out=s_vec[:, TSL:2 * TSL],
                                      in_=s_vec[:, 2 * TSL:3 * TSL],
                                      func=AF.Ln)
                    scalar.activation(out=s_vec[:, 2 * TSL:3 * TSL],
                                      in_=s_vec[:, TSL:2 * TSL],
                                      func=AF.Exp, scale=-0.5)
                    scalar.activation(out=s_vec[:, 2 * TSL:3 * TSL],
                                      in_=s_vec[:, TSL:2 * TSL],
                                      func=AF.Exp, scale=-0.5).then_inc(sse, 1)

            with nc.Block() as blk:
                @blk.tensor
                def _(tensor):
                    for rnd in range(3):
                        for h in range(TSL // 512):
                            sl = slice(h * 512, (h + 1) * 512)
                            tensor.matmul(p_rp[:, sl], s_omf[:],
                                          s_vec[0:1, h * 512:(h + 1) * 512],
                                          start=True, stop=True)
                            tensor.matmul(p_rp2[:, sl], s_omf[:],
                                          s_vec[0:1, 2 * TSL + h * 512:
                                                2 * TSL + (h + 1) * 512],
                                          start=True, stop=True)

            with nc.Block() as blk:
                @blk.vector
                def _(vector):
                    vector.tensor_copy(out=s_mi[:, 0:TSL], in_=p_rp[:])
                    vector.tensor_copy(out=s_mi[:, TSL:2 * TSL], in_=p_rp2[:])
                    for gg in range(NGG):
                        if mode == "plain":
                            vector.tensor_sub(out=s_x1[gg][:], in0=s_x1[gg][:],
                                              in1=s_mi[:, 0:TSL])
                            ins = vector.tensor_mul(out=s_x1[gg][:],
                                                    in0=s_x1[gg][:],
                                                    in1=s_mi[:, TSL:2 * TSL])
                            if fuse_next is not None:
                                ins.then_inc(lnr, 1)
                                if gg >= 3:
                                    vector.wait_ge(fte, offn + gg - 2)
                                ins = vector.tensor_copy(
                                    out=s_r16[gg % 3][:],
                                    in_=s_x1[gg][:])
                                ins.then_inc(fca, 1)
                            elif debug:
                                ins.then_inc(dbgs, 1)
                        elif mode == "affine":
                            vector.tensor_sub(out=s_yr[gg % 2][:],
                                              in0=s_x1[gg][:], in1=s_mi[:, 0:TSL])
                            vector.tensor_mul(out=s_yr[gg % 2][:],
                                              in0=s_yr[gg % 2][:], in1=s_mi[:, TSL:2 * TSL])
                            vector.tensor_scalar(
                                out=s_xn[gg][:], in0=s_yr[gg % 2][:],
                                scalar1=s_lnw[:, gg:gg + 1],
                                scalar2=s_lnb[:, gg:gg + 1],
                                op0=ALU.mult, op1=ALU.add)
                        else:
                            if gg >= 2:
                                vector.wait_ge(ost, (gg - 1) * 16)
                            vector.tensor_sub(out=s_x1[gg][:], in0=s_x1[gg][:],
                                              in1=s_mi[:, 0:TSL])
                            vector.tensor_mul(out=s_ost[gg % 2][:],
                                              in0=s_x1[gg][:],
                                              in1=s_mi[:, TSL:2 * TSL]).then_inc(fvv, 1)
                    if fuse_next is not None:
                        vector.tensor_copy(out=s_q16d[:],
                                           in_=s_x1[0][:, 0:512]
                                           ).then_inc(fca, 1)
                        vector.tensor_copy(out=s_q16d[:],
                                           in_=s_x1[0][:, 0:512]
                                           ).then_inc(lnr, 1)

                if fuse_next is not None:
                    p_s1n, p_s2n = fuse_next

                    @blk.scalar
                    def _(scalar):
                        for gg in range(NGG):
                            scalar.wait_ge(lnr, min(gg + 2, NGG + 1))
                            if gg >= 3:
                                scalar.wait_ge(fte, offn + gg - 2)
                            scalar.activation(
                                out=s_q16[gg % 3][:], in_=s_x1[gg][:],
                                func=AF.Square).then_inc(fsq, 1)
                        scalar.activation(out=s_q16d[:], in_=s_x1[0][:, 0:512],
                                          func=AF.Square).then_inc(fsq, 1)

                    @blk.tensor
                    def _(tensor):
                        for gg in range(NGG):
                            tensor.wait_ge(fsq, off17n + min(gg + 2, NGG + 1))
                            tensor.wait_ge(fca, off17n + min(gg + 2, NGG + 1))
                            for h in range(TSL // 512):
                                sl = slice(h * 512, (h + 1) * 512)
                                tensor.matmul(p_s1n[0:1, sl], s_ok16[:],
                                              s_r16[gg % 3][:, sl],
                                              start=(gg == 0), stop=False)
                                ins = tensor.matmul(p_s2n[0:1, sl], s_ok16[:],
                                                    s_q16[gg % 3][:, sl],
                                                    start=(gg == 0), stop=False)
                                if h == TSL // 512 - 1:
                                    ins.then_inc(fte, 1)
                        for rnd in range(2):
                            for h2 in range(TSL // 512):
                                sl2 = slice(h2 * 512, (h2 + 1) * 512)
                                tensor.matmul(p_s1n[0:1, sl2], s_ok16[:],
                                              s_z16[:], start=False,
                                              stop=(rnd == 1))
                                tensor.matmul(p_s2n[0:1, sl2], s_ok16[:],
                                              s_z16[:], start=False,
                                              stop=(rnd == 1))

                if mode == "final":
                    FVV0 = NGG + NGG * (TSL // 512)

                    @blk.sync
                    def _(sync):
                        for gg in range(NGG):
                            sync.wait_ge(fvv, FVV0 + gg + 1)
                            sync.dma_start(out=yout[gg * 128:(gg + 1) * 128, :],
                                           in_=s_ost[gg % 2][:]).then_inc(ost, 16)

                    @blk.gpsimd
                    def _(g):
                        g.wait_ge(ost, NGG * 16)

    emit_ln(0, "plain", ext_stats=(p_s1_0, p_s2_0, es0),
            fuse_next=(p_s1_1x, p_s2_1x))
    emit_ln(1, "affine", ext_stats=(p_s1_1x, p_s2_1x, es1))
    if debug:
        with nc.Block() as blk:
            @blk.gpsimd
            def _(g):
                for gg in range(NGG):
                    g.dma_start(out=dbg_x1[gg], in_=s_x1[gg][:]).then_inc(yld, 16)
    if debug:
        with nc.Block() as blk:
            @blk.gpsimd
            def _(g):
                dsem = nc.semaphore("dsem5").__enter__()
                for gg in range(NGG):
                    g.dma_start(out=dbg_xn[gg], in_=s_xn[gg][:]).then_inc(dsem, 16)
                g.wait_ge(dsem, NGG * 16)

    pfs = ExitStack()
    p_f = [pfs.enter_context(nc.psum_tensor(f"p_f{i}", [128, 512], F32))
           for i in range(2)]
    with nc.Block() as blk:
        @blk.tensor
        def _(tensor):
            tensor.wait_ge(wql, NGG * 16)
            for og in range(NGG):
                for th in range(TSL // 512):
                    m = og * (TSL // 512) + th
                    pt = p_f[m % 2]
                    if m >= 2:
                        tensor.wait_ge(fvv, NGG + m - 1)
                    for k in range(NGG):
                        tensor.matmul(
                            pt[:], s_Wq[k][:, og * 128:(og + 1) * 128],
                            s_xn[k][:, th * 512:(th + 1) * 512],
                            start=(k == 0), stop=False)
                    tensor.matmul(pt[:], s_Wq[0][:, 0:128], s_z16[:],
                                  start=False, stop=False)
                    tensor.matmul(pt[:], s_Wq[0][:, 0:128], s_z16[:],
                                  start=False, stop=True).then_inc(fmm, 1)

        @blk.vector
        def _(vector):
            for og in range(NGG):
                for th in range(TSL // 512):
                    m = og * (TSL // 512) + th
                    vector.wait_ge(fmm, m + 1)
                    sl = slice(th * 512, (th + 1) * 512)
                    vector.scalar_tensor_tensor(
                        out=s_x1[og][:, sl], in0=s_x1[og][:, sl], scalar=1.0,
                        in1=p_f[m % 2][:], op0=ALU.mult, op1=ALU.add,
                    ).then_inc(fvv, 1)

    if debug:
        with nc.Block() as blk:
            @blk.gpsimd
            def _(g):
                dsem = nc.semaphore("dsem6").__enter__()
                for gg in range(NGG):
                    g.dma_start(out=dbg_r2[gg], in_=s_x1[gg][:]).then_inc(dsem, 16)
                g.wait_ge(dsem, NGG * 16)

    pfs.close()
    emit_ln(2, "final")

    if debug:
        with nc.Block() as blk2:
            @blk2.gpsimd
            def _(g):
                dsem7 = nc.semaphore("dsem7").__enter__()
                g.dma_start(out=dbg_v, in_=s_vec[:]).then_inc(dsem7, 16)
                g.dma_start(out=dbg_mi, in_=s_mi[:]).then_inc(dsem7, 16)
                g.wait_ge(dsem7, 32)

    ff.close()
    ex.close()
    return nc


_NC_CACHE = None


def kernel(L0_attn_x, L0_attn_z, L0_attn_W_bc, L0_attn_W_dt, L0_attn_b_dt,
           L0_attn_A_log, L0_attn_D_param, L0_ffn_W, L0_ffn_ln_w, L0_ffn_ln_b):
    global _NC_CACHE
    import ml_dtypes
    bf = ml_dtypes.bfloat16

    x2d = np.ascontiguousarray(np.asarray(L0_attn_x, dtype=np.float32)
                               .reshape(T, D))
    W_bc = np.asarray(L0_attn_W_bc, dtype=np.float32)
    W_dt = np.asarray(L0_attn_W_dt, dtype=np.float32)
    b_dt = np.asarray(L0_attn_b_dt, dtype=np.float32)
    A_log = np.asarray(L0_attn_A_log, dtype=np.float32)
    D_param = np.asarray(L0_attn_D_param, dtype=np.float32)
    W = np.ascontiguousarray(np.asarray(L0_ffn_W, dtype=np.float32))
    ln_w = np.asarray(L0_ffn_ln_w, dtype=np.float32)
    ln_b = np.asarray(L0_ffn_ln_b, dtype=np.float32)

    ident = np.eye(128, dtype=np.float32)
    in_maps = []
    for c in range(NCORES):
        dsl = slice(c * DSL, (c + 1) * DSL)
        tsl = slice(c * TSL, (c + 1) * TSL)
        in_maps.append(dict(
            xdsT=np.ascontiguousarray(x2d[:, dsl].T),
            xtsT=np.ascontiguousarray(x2d[tsl, :].T),
            wbcT=np.ascontiguousarray(W_bc[:, dsl].T),
            wdtT=np.ascontiguousarray(W_dt[dsl, :].T),
            alog=np.ascontiguousarray(A_log[dsl, :]),
            bdt=np.ascontiguousarray(b_dt[dsl].reshape(NDG, 128).T),
            dpar=np.ascontiguousarray(D_param[dsl].reshape(NDG, 128).T),
            lnw=np.ascontiguousarray(ln_w.reshape(NGG, 128).T),
            lnb=np.ascontiguousarray(ln_b.reshape(NGG, 128).T),
            wmat=W,
            ident16=ident.astype(bf),
            ones_k16=np.ones((128, 1), dtype=bf),
            ones_kf=np.ones((128, 1), dtype=np.float32),
            ones_mf=np.ones((1, 128), dtype=np.float32),
            zer16=np.zeros((128, 512), dtype=bf),
            zerf=np.zeros((128, 1), dtype=np.float32),
        ))

    if _NC_CACHE is None:
        _NC_CACHE = build_nc()
    res = run_bass_kernel_spmd(_NC_CACHE, in_maps, core_ids=list(range(NCORES)))

    out = np.empty((1, T, D), dtype=np.float32)
    for c in range(NCORES):
        out[0, c * TSL:(c + 1) * TSL, :] = res.results[c]["yout"].T
    return out
